# version 02
# baseline (speedup 1.0000x reference)
import numpy as np
import ml_dtypes
import concourse.bass as bass
import concourse.mybir as mybir
from concourse.bass_utils import run_bass_kernel_spmd

F32 = mybir.dt.float32
BF16 = mybir.dt.bfloat16
I32 = mybir.dt.int32
U8 = mybir.dt.uint8
AF = mybir.ActivationFunctionType
ALU = mybir.AluOpType
BF = ml_dtypes.bfloat16

B, S, H, NH, NKV, D, I = 4, 4096, 4096, 32, 8, 128, 11008
TOK = 2048
NF = H // 128
NI = I // 128
M_EXP = 30.0
QK_SCALE = D ** -0.5
SLOPES = [2.0 ** (-0.25 * (hq + 1)) for hq in range(NH)]
MYBLOCKS = [[0, 3, 4, 7], [1, 2, 5, 6]]
PADDED = [8, 16, 24, 32]
CUMPAD = [0, 8, 24, 48]
NKT = 80
PAT = [[0, 4, 1, 5], [2, 6, 3, 7]]
EPS = 1e-5
POOL_LIM = 180224

ENGINES = ("sync", "vector", "scalar", "tensor", "gpsimd")


def _perm(n_j):
    n = n_j * 256
    p = np.empty(n, dtype=np.int64)
    for plane in range(2):
        for j in range(n_j):
            t = np.arange(128)
            p[plane * n_j * 128 + 128 * j + t] = 8 * (32 * j + t // 4) + np.array(PAT[plane])[t % 4]
    return p


PERM_H = _perm(16)
PERM_I = _perm(43)
CONSTS = [-M_EXP, EPS, -0.5, 1.0 / 4096.0, QK_SCALE, 1.0, 0.0, -1.0]


class Bld:
    def __init__(self, nc, sems):
        self.nc = nc
        self.sems = sems
        self.streams = {e: [] for e in ENGINES}
        self.cnt = {k: 0 for k in sems}

    def raw(self, eng, fn):
        self.streams[eng].append(fn)

    def wait(self, eng, sem, val):
        if val is None or val <= 0:
            return
        s = self.sems[sem]
        self.raw(eng, lambda e, s=s, v=val: e.wait_ge(s, v))

    def dma(self, eng, sem, out, in_):
        s = self.sems[sem]
        self.cnt[sem] += 16
        tgt = self.cnt[sem]
        self.raw(eng, lambda e, o=out, i=in_, s=s: e.dma_start(out=o, in_=i).then_inc(s, 16))
        return tgt

    def op(self, eng, meth, *a, sem=None, **kw):
        s = tgt = None
        if sem is not None:
            s = self.sems[sem]
            self.cnt[sem] += 1
            tgt = self.cnt[sem]

        def f(e, meth=meth, a=a, kw=kw, s=s):
            inst = getattr(e, meth)(*a, **kw)
            if s is not None:
                inst.then_inc(s, 1)
        self.raw(eng, f)
        return tgt

    def mm(self, out, lhsT, rhs, start, stop, sem=None):
        return self.op("tensor", "matmul", out, lhsT, rhs, start=start, stop=stop, sem=sem)

    def barrier(self):
        snap = dict(self.cnt)
        for e in ENGINES:
            for k, v in snap.items():
                self.wait(e, k, v)


def ceil4(x):
    return (x + 3) & ~3


class Carve:
    def __init__(self, pool):
        self.pool = pool
        self.off = 0

    def take(self, nelem, dtype, rows=128):
        size = {F32: 4, BF16: 2, I32: 4, U8: 1}[dtype]
        nbytes = ceil4(nelem * size)
        assert self.off + nbytes <= POOL_LIM, f"pool overflow {self.off + nbytes}"
        v = self.pool[0:rows, self.off:self.off + nbytes]
        if dtype != U8:
            v = v.bitcast(dtype)
        self.off += nbytes
        return v


def build_program(upto=8, dbg=False):
    nc = bass.Bass("TRN2", target_bir_lowering=False)
    dt = nc.dram_tensor
    DK = "ExternalOutput" if dbg else "Internal"

    x_batchT = dt("x_batchT", [H, S], F32, kind="ExternalInput").ap()
    x_myT = dt("x_myT", [H, TOK], F32, kind="ExternalInput").ap()
    wqkv_t = dt("wqkv_t", [40, 32, 128, 128], F32, kind="ExternalInput").ap()
    wv_t = dt("wv_t", [32, 128, 1024], F32, kind="ExternalInput").ap()
    wo_t = dt("wo_t", [32, 32, 128, 128], F32, kind="ExternalInput").ap()
    qwg_t = dt("qwg_t", [43, 128, 1024], I32, kind="ExternalInput").ap()
    qwu_t = dt("qwu_t", [43, 128, 1024], I32, kind="ExternalInput").ap()
    qwd_t = dt("qwd_t", [86, 128, 512], I32, kind="ExternalInput").ap()
    sg_dev = dt("sg_dev", [32, I], F32, kind="ExternalInput").ap()
    su_dev = dt("su_dev", [32, I], F32, kind="ExternalInput").ap()
    sgB_t = dt("sgB_t", [86, 2, 4096], F32, kind="ExternalInput").ap()
    suB_t = dt("suB_t", [86, 2, 4096], F32, kind="ExternalInput").ap()
    sd_dev = dt("sd_dev", [86, H], F32, kind="ExternalInput").ap()
    qzg = dt("qzg", [32, 1376], I32, kind="ExternalInput").ap()
    qzu = dt("qzu", [32, 1376], I32, kind="ExternalInput").ap()
    qzd = dt("qzd", [86, 512], I32, kind="ExternalInput").ap()
    sdm_t = dt("sdm_t", [NKT, 128, 512], BF16, kind="ExternalInput").ap()
    b2_in = dt("b2_in", [128, 512], BF16, kind="ExternalInput").ap()
    bd_in = dt("bd_in", [128, 43 * 86], BF16, kind="ExternalInput").ap()
    id_in = dt("id_in", [128, 128], BF16, kind="ExternalInput").ap()
    cst_in = dt("cst_in", [1, 16], F32, kind="ExternalInput").ap()

    outT = dt("outT", [H, TOK], F32, kind="ExternalOutput").ap()

    kspill = dt("kspill", [NKV * D, S], BF16, kind=DK).ap()
    vspill = dt("vspill", [S, NKV * D], BF16, kind=DK).ap()
    qspill = dt("qspill", [NH * D, TOK], BF16, kind=DK).ap()
    ospill = dt("ospill", [NH * D, TOK], BF16, kind=DK).ap()
    xln1sp = dt("xln1sp", [H, S], BF16, kind=DK).ap()
    out1sp = dt("out1sp", [H, TOK], F32, kind=DK).ap()
    xln2sp = dt("xln2sp", [H, TOK], BF16, kind=DK).ap()
    hspill = dt("hspill", [I, TOK], BF16, kind=DK).ap()
    wgsp = dt("wgsp", [NI, 128, 4096], BF16, kind=DK).ap()
    wusp = dt("wusp", [NI, 128, 4096], BF16, kind=DK).ap()
    wdsp = dt("wdsp", [32, 128, NI * 128], BF16, kind=DK).ap()
    sgbf = dt("sgbf", [32, I], BF16, kind="Internal").ap()
    subf = dt("subf", [32, I], BF16, kind="Internal").ap()
    sgBbf = dt("sgBbf", [86, 2, 4096], BF16, kind="Internal").ap()
    suBbf = dt("suBbf", [86, 2, 4096], BF16, kind="Internal").ap()
    sdbf = dt("sdbf", [86, H], BF16, kind="Internal").ap()
    rowsp = dt("rowsp", [1, 1024], F32, kind="Internal").ap()
    xs2dump = dt("xs2dump", [32, TOK], BF16, kind=DK).ap()
    hdump = dt("hdump", [128, 4096], BF16, kind=DK).ap()
    vg_d = dt("vg_d", [32, I], BF16, kind=DK).ap()
    vu_d = dt("vu_d", [32, I], BF16, kind=DK).ap()
    vd_d = dt("vd_d", [86, H], BF16, kind=DK).ap()

    from contextlib import ExitStack
    with ExitStack() as stack:
        ec = stack.enter_context
        cst = ec(nc.sbuf_tensor("cst", [128, 16], F32))
        idbf = ec(nc.sbuf_tensor("idbf", [128, 128], BF16))
        onec_bf = ec(nc.sbuf_tensor("onec_bf", [128, 1], BF16))
        oner_f = ec(nc.sbuf_tensor("oner_f", [1, 128], F32))
        onerb = ec(nc.sbuf_tensor("onerb", [1, 128], BF16))
        rows_bf = ec(nc.sbuf_tensor("rows_bf", [1, 2048], BF16))
        rows = ec(nc.sbuf_tensor("rows", [1, 3072], F32))
        b2sb = ec(nc.sbuf_tensor("b2sb", [128, 512], BF16))
        bdsb = ec(nc.sbuf_tensor("bdsb", [128, 43 * 86], BF16))
        xs2sb = ec(nc.sbuf_tensor("xs2sb", [32, TOK], BF16))
        hssb = ec(nc.sbuf_tensor("hssb", [86, 512], BF16))
        vgt = ec(nc.sbuf_tensor("vgt", [32, 256], BF16))
        vut = ec(nc.sbuf_tensor("vut", [32, 256], BF16))
        vdt = ec(nc.sbuf_tensor("vdt", [86, 256], BF16))
        pool = ec(nc.sbuf_tensor("pool", [128, POOL_LIM], U8))
        psA = ec(nc.psum_tensor("psA", [128, 1024], F32))
        psB = ec(nc.psum_tensor("psB", [128, 1024], F32))
        ps0 = ec(nc.psum_tensor("ps0", [128, 512], F32))
        ps1 = ec(nc.psum_tensor("ps1", [128, 512], F32))
        ps2 = ec(nc.psum_tensor("ps2", [128, 512], F32))
        ps3 = ec(nc.psum_tensor("ps3", [128, 512], F32))
        _sW = ec(nc.semaphore("sW"))
        _sW2 = ec(nc.semaphore("sW2"))
        _sX = ec(nc.semaphore("sX"))
        _sX2 = ec(nc.semaphore("sX2"))
        _sX3 = ec(nc.semaphore("sX3"))
        _sX4 = ec(nc.semaphore("sX4"))
        _sXS = ec(nc.semaphore("sXS"))
        _sWa = ec(nc.semaphore("sWa"))
        _sWb = ec(nc.semaphore("sWb"))
        _sQa = ec(nc.semaphore("sQa"))
        _sQb = ec(nc.semaphore("sQb"))
        _sSa = ec(nc.semaphore("sSa"))
        _sSb = ec(nc.semaphore("sSb"))
        _sPE = ec(nc.semaphore("sPE"))
        _sACT = ec(nc.semaphore("sACT"))
        _sDVE = ec(nc.semaphore("sDVE"))
        block = ec(nc.Block())
        for ci, v in enumerate(CONSTS):
            nc.const_aps.register(F32, v, cst[:, ci:ci + 1])

        sems = {"sW": _sW, "sW2": _sW2, "sX": _sX, "sX2": _sX2, "sX3": _sX3,
                "sX4": _sX4, "sXS": _sXS, "sWa": _sWa, "sWb": _sWb,
                "sQa": _sQa, "sQb": _sQb, "sSa": _sSa, "sSb": _sSb,
                "sPE": _sPE, "sACT": _sACT, "sDVE": _sDVE}
        b = Bld(nc, sems)

        def row(i):
            return rows[0:1, i * 512:(i + 1) * 512]
        r_sum, r_sq, r_m, r_t1, r_rstd, r_mr = (row(i) for i in range(6))

        b.dma("sync", "sW", cst[:], cst_in.broadcast_to([128, 16]))
        b.dma("sync", "sW", idbf[:], id_in)
        b.dma("sync", "sW", b2sb[:], b2_in)
        t0 = b.dma("sync", "sW", bdsb[:], bd_in)
        b.wait("vector", "sW", t0)
        b.op("vector", "memset", onec_bf[:], 1.0)
        b.op("vector", "memset", onerb[:], 1.0)
        b.op("vector", "memset", oner_f[:], 1.0, sem="sDVE")

        cv = Carve(pool)
        p0_dma = 0
        zsb = cv.take(5504, U8, rows=86)
        zu8 = cv.take(11008, U8, rows=86)
        ssb = cv.take(44032, U8, rows=86)
        obf = cv.take(22016, U8, rows=86)
        vsb = cv.take(22016, U8, rows=86)

        for (qz, sdev, sbf_dst, v_dst, nrow, ncol) in (
            (qzg, sg_dev, sgbf, vg_d, 32, I),
            (qzu, su_dev, subf, vu_d, 32, I),
            (qzd, sd_dev, sdbf, vd_d, 86, H),
        ):
            npk = ncol // 8
            half = ncol // 2
            b.wait("sync", "sDVE", b.cnt["sDVE"])
            tz = b.dma("sync", "sW", zsb[0:nrow, 0:npk * 4].bitcast(I32), qz)
            ts = b.dma("sync", "sW", ssb[0:nrow, 0:ncol * 4].bitcast(F32), sdev)
            b.wait("vector", "sW", max(tz, ts))
            b.wait("vector", "sXS", p0_dma)
            zz = zsb[0:nrow, 0:npk * 4]
            b.op("vector", "tensor_single_scalar", zu8[0:nrow, 0:half], zz, 15, ALU.bitwise_and)
            b.op("vector", "tensor_single_scalar", zu8[0:nrow, half:ncol], zz, 4, ALU.logical_shift_right)
            sview = ssb[0:nrow, 0:ncol * 4].bitcast(F32)
            b.op("vector", "tensor_copy", obf[0:nrow, 0:ncol * 2].bitcast(BF16), sview)
            b.op("vector", "tensor_tensor", vsb[0:nrow, 0:ncol * 2].bitcast(BF16),
                 zu8[0:nrow, 0:ncol], sview, ALU.mult)
            vv = b.op("vector", "tensor_scalar_mul", vsb[0:nrow, 0:ncol * 2].bitcast(BF16),
                      vsb[0:nrow, 0:ncol * 2].bitcast(BF16), -1.0, sem="sDVE")
            b.wait("gpsimd", "sDVE", vv)
            b.dma("gpsimd", "sXS", sbf_dst, obf[0:nrow, 0:ncol * 2].bitcast(BF16))
            p0_dma = b.dma("gpsimd", "sXS", v_dst, vsb[0:nrow, 0:ncol * 2].bitcast(BF16))
        for (srcB, dstB) in ((sgB_t, sgBbf), (suB_t, suBbf)):
            b.wait("sync", "sDVE", b.cnt["sDVE"])
            tb = b.dma("sync", "sW", ssb[0:86, 0:32768].bitcast(F32), srcB.rearrange("m h n -> m (h n)"))
            b.wait("vector", "sW", tb)
            b.wait("vector", "sXS", p0_dma)
            vv = b.op("vector", "tensor_copy", obf[0:86, 0:16384].bitcast(BF16),
                      ssb[0:86, 0:32768].bitcast(F32), sem="sDVE")
            b.wait("gpsimd", "sDVE", vv)
            p0_dma = b.dma("gpsimd", "sXS", dstB.rearrange("m h n -> m (h n)"),
                           obf[0:86, 0:16384].bitcast(BF16))
        b.barrier()

        def ln_rows(pe_stats, RTMT, rtmt_guard, apply_guard=0):
            b.wait("scalar", "sPE", pe_stats)
            b.op("scalar", "copy", r_sum, ps2[0:1, :])
            a1 = b.op("scalar", "copy", r_sq, ps3[0:1, :], sem="sACT")
            b.wait("vector", "sACT", a1)
            b.op("vector", "tensor_scalar_mul", r_m, r_sum, 1.0 / 4096.0)
            b.op("vector", "tensor_scalar_mul", r_t1, r_sq, 1.0 / 4096.0)
            b.op("vector", "tensor_mul", r_sum, r_m, r_m)
            v1 = b.op("vector", "tensor_sub", r_t1, r_t1, r_sum, sem="sDVE")
            b.wait("scalar", "sDVE", v1)
            b.op("scalar", "activation", r_sq, r_t1, AF.Ln, bias=EPS, scale=1.0)
            a2 = b.op("scalar", "activation", r_rstd, r_sq, AF.Exp, bias=0.0, scale=-0.5, sem="sACT")
            b.wait("vector", "sACT", a2)
            b.op("vector", "tensor_mul", r_mr, r_m, r_rstd)
            b.op("vector", "tensor_copy", rows_bf[0:1, 0:512], r_rstd)
            b.op("vector", "tensor_sub", r_sum, r_rstd, rows_bf[0:1, 0:512])
            b.op("vector", "tensor_copy", rows_bf[0:1, 512:1024], r_sum)
            b.op("vector", "tensor_copy", rows_bf[0:1, 1024:1536], r_mr)
            b.op("vector", "tensor_sub", r_sum, r_mr, rows_bf[0:1, 1024:1536])
            v2 = b.op("vector", "tensor_copy", rows_bf[0:1, 1536:2048], r_sum, sem="sDVE")
            b.wait("tensor", "sDVE", v2)
            b.wait("tensor", "sACT", rtmt_guard)
            b.mm(psA[:, 0:512], onerb[:], rows_bf[0:1, 0:512], start=True, stop=False)
            b.mm(psA[:, 0:512], onerb[:], rows_bf[0:1, 512:1024], start=False, stop=True)
            b.mm(psA[:, 512:1024], onerb[:], rows_bf[0:1, 1024:1536], start=True, stop=False)
            p1 = b.mm(psA[:, 512:1024], onerb[:], rows_bf[0:1, 1536:2048], start=False, stop=True, sem="sPE")
            b.wait("scalar", "sPE", p1)
            b.wait("scalar", "sDVE", apply_guard)
            b.op("scalar", "copy", RTMT[:, 0:512], psA[:, 0:512])
            a3 = b.op("scalar", "copy", RTMT[:, 512:1024], psA[:, 512:1024], sem="sACT")
            return a3, a1

        def ln_proj(xsrc, nsup, oclist, spill_scale, spill_ap, xlnspill):
            cv = Carve(pool)
            XLNP = [cv.take(NF * 512, BF16) for _ in range(2)]
            WST = cv.take(NF * 128, F32)
            WBF = [cv.take(NF * 128, BF16) for _ in range(2)]
            XC = [cv.take(512, F32) for _ in range(4)]
            XBF = [cv.take(512, BF16) for _ in range(2)]
            SQB = [cv.take(512, BF16) for _ in range(2)]
            TTB = [cv.take(512, F32) for _ in range(2)]
            RTMT = cv.take(1024, F32)
            RT = RTMT[:, 0:512]
            MT = RTMT[:, 512:1024]
            EVB = [cv.take(512, BF16) for _ in range(2)]

            pe_stat = [0, 0]
            xc_free = [0, 0, 0, 0]
            evac_done = [0, 0]
            rtmt_guard = 0
            last_apply = 0
            rows_free = 0
            wcast_last = 0
            wcast = [0, 0]
            pe_w = [0, 0]
            evb_dma = [0, 0]
            pe_pair_done = 0

            for pair in range((nsup + 1) // 2):
                sps = list(range(2 * pair, min(2 * pair + 2, nsup)))
                xln_ready = [0, 0]
                for spi, sp in enumerate(sps):
                    ts0 = 512 * sp
                    for fc in range(NF):
                        par = fc % 2
                        xsem = "sQa" if par == 0 else "sQb"
                        b.wait("gpsimd", "sDVE", xc_free[par])
                        tx = b.dma("gpsimd", xsem, XC[par], xsrc[128 * fc:128 * fc + 128, ts0:ts0 + 512])
                        b.wait("vector", xsem, tx)
                        b.wait("vector", "sPE", pe_stat[par])
                        vd = b.op("vector", "tensor_copy", XBF[par], XC[par], sem="sDVE")
                        xc_free[par] = vd
                        b.wait("scalar", "sDVE", vd)
                        ad = b.op("scalar", "activation", SQB[par], XBF[par], AF.Square, sem="sACT")
                        b.wait("tensor", "sACT", ad)
                        if fc == 0:
                            b.wait("tensor", "sACT", rows_free)
                        b.mm(ps2[0:1, :], onec_bf[:], XBF[par], start=(fc == 0), stop=(fc == NF - 1))
                        pe_stat[par] = b.mm(ps3[0:1, :], onec_bf[:], SQB[par],
                                            start=(fc == 0), stop=(fc == NF - 1), sem="sPE")
                    t_rtmt, rows_free = ln_rows(pe_stat[(NF - 1) % 2], RTMT, rtmt_guard,
                                                 apply_guard=last_apply)
                    rtmt_guard = t_rtmt
                    for fc in range(NF):
                        par = fc % 2
                        xsem = "sWa" if par == 0 else "sWb"
                        b.wait("gpsimd", "sDVE", xc_free[2 + par])
                        tx = b.dma("gpsimd", xsem, XC[2 + par], xsrc[128 * fc:128 * fc + 128, ts0:ts0 + 512])
                        b.wait("vector", xsem, tx)
                        b.wait("vector", "sACT", t_rtmt)
                        if fc < 2:
                            b.wait("vector", "sPE", pe_pair_done)
                        b.op("vector", "tensor_mul", TTB[par], XC[2 + par], RT)
                        xc_free[2 + par] = b.cnt["sDVE"] + 1
                        xsl = XLNP[spi][:, fc * 512:(fc + 1) * 512]
                        vv = b.op("vector", "tensor_sub", xsl, TTB[par], MT, sem="sDVE")
                        if xlnspill:
                            b.wait("gpsimd", "sDVE", vv)
                            b.dma("gpsimd", "sXS", xln1sp[128 * fc:128 * fc + 128, ts0:ts0 + 512], xsl)
                    xln_ready[spi] = last_apply = b.cnt["sDVE"]
                psK = [ps0, ps1]
                for oci, oc in enumerate(oclist):
                    par = oci % 2
                    b.wait("sync", "sDVE", wcast_last)
                    tw = b.dma("sync", "sW", WST.rearrange("p (f c) -> p f c", c=128), wqkv_t[oc].rearrange("f p c -> p f c"))
                    b.wait("vector", "sW", tw)
                    b.wait("vector", "sPE", pe_w[par])
                    wcast[par] = wcast_last = b.op("vector", "tensor_copy", WBF[par], WST, sem="sDVE")
                    for spi, sp in enumerate(sps):
                        ts0 = 512 * sp
                        b.wait("tensor", "sDVE", max(wcast[par], xln_ready[spi]))
                        b.wait("tensor", "sACT", evac_done[par])
                        pe = None
                        for f in range(NF):
                            last = (f == NF - 1)
                            pe = b.mm(psK[par][:], WBF[par][:, f * 128:(f + 1) * 128],
                                      XLNP[spi][:, f * 512:(f + 1) * 512],
                                      start=(f == 0), stop=last, sem=("sPE" if last else None))
                        pe_w[par] = pe
                        b.wait("scalar", "sPE", pe)
                        b.wait("scalar", "sSa" if par == 0 else "sSb", evb_dma[par])
                        if spill_scale is None:
                            aev = b.op("scalar", "copy", EVB[par], psK[par][:], sem="sACT")
                        else:
                            aev = b.op("scalar", "activation", EVB[par], psK[par][:], AF.Copy,
                                       bias=0.0, scale=spill_scale, sem="sACT")
                        evac_done[par] = aev
                        b.wait("gpsimd", "sACT", aev)
                        evb_dma[par] = b.dma("gpsimd", "sSa" if par == 0 else "sSb",
                                             spill_ap(oc, ts0), EVB[par])
                pe_pair_done = b.cnt["sPE"]
            b.barrier()

        if upto >= 1:
            ln_proj(x_batchT, 8, list(range(32, 40)), None,
                    lambda oc, ts0: kspill[128 * (oc - 32):128 * (oc - 32) + 128, ts0:ts0 + 512],
                    xlnspill=True)

        if upto >= 2:
            cv = Carve(pool)
            XLN1 = cv.take(NF * 512, BF16)
            WVST = [cv.take(1024, F32) for _ in range(2)]
            WVBF = cv.take(NF * 1024, BF16)
            VEV = [cv.take(512, BF16) for _ in range(2)]
            wcast = 0
            wv_casts = [0, 0]
            for f in range(NF):
                par = f % 2
                b.wait("sync", "sDVE", wv_casts[par])
                tw = b.dma("sync", "sW", WVST[par], wv_t[f])
                b.wait("vector", "sW", tw)
                wcast = wv_casts[par] = b.op(
                    "vector", "tensor_copy",
                    WVBF[:, f * 1024:(f + 1) * 1024], WVST[par], sem="sDVE")
            evac_done = [0, 0]
            vev_dma = [0, 0]
            for sp in range(8):
                ts0 = 512 * sp
                b.wait("gpsimd", "sPE", b.cnt["sPE"])
                xdma = b.dma("gpsimd", "sX", XLN1.rearrange("p (f q) -> p f q", q=512),
                             xln1sp[:, ts0:ts0 + 512].rearrange("(f p) q -> p f q", p=128))
                pi = 0
                for tc in range(4):
                    for vc in range(2):
                        par = pi % 2
                        pi += 1
                        b.wait("tensor", "sX", xdma)
                        b.wait("tensor", "sDVE", wcast)
                        b.wait("tensor", "sACT", evac_done[par])
                        pe = None
                        for f in range(NF):
                            last = (f == NF - 1)
                            pe = b.mm([ps0, ps1][par][:],
                                      XLN1[:, f * 512 + tc * 128: f * 512 + tc * 128 + 128],
                                      WVBF[:, f * 1024 + vc * 512: f * 1024 + vc * 512 + 512],
                                      start=(f == 0), stop=last, sem=("sPE" if last else None))
                        b.wait("scalar", "sPE", pe)
                        b.wait("scalar", "sSa" if par == 0 else "sSb", vev_dma[par])
                        aev = b.op("scalar", "copy", VEV[par], [ps0, ps1][par][:], sem="sACT")
                        evac_done[par] = aev
                        b.wait("gpsimd", "sACT", aev)
                        vev_dma[par] = b.dma(
                            "gpsimd", "sSa" if par == 0 else "sSb",
                            vspill[ts0 + tc * 128: ts0 + tc * 128 + 128, vc * 512:vc * 512 + 512],
                            VEV[par])
            b.barrier()

        if upto >= 3:
            ln_proj(x_myT, 4, list(range(32)), QK_SCALE,
                    lambda oc, ts0: qspill[128 * oc:128 * oc + 128, ts0:ts0 + 512],
                    xlnspill=False)

        if upto >= 4:
            import os as _os
            P4P = int(_os.environ.get("P4P", "4"))
            P4G = int(_os.environ.get("P4G", "8"))
            P4R = int(_os.environ.get("P4R", "2"))
            P4MODE = int(_os.environ.get("P4MODE", "0"))
            cv = Carve(pool)
            SDMB = cv.take(32 * 512, BF16)
            QPR = [cv.take(1024, BF16) for _ in range(2)]
            KALL = [cv.take(32 * 128, BF16) for _ in range(2)]
            VALL = [cv.take(32 * 128, BF16) for _ in range(2)]
            TMP = [cv.take(1024, BF16) for _ in range(2)]
            EALL = cv.take(32 * 1024, BF16)
            OSB = cv.take(NH * 512, BF16)
            OTb = [cv.take(512, F32) for _ in range(2)]
            RTb = [cv.take(512, F32) for _ in range(2)]

            sdm_free = 0
            osb_free = 0
            pvO_free = 0
            ot_free = 0
            st_free = [0, 0]
            kv_free = [0, 0]
            qpr_free = [0, 0]
            eall_free = 0
            psST = [psA, psB]
            for p in range(P4P):
                n = PADDED[p]
                b.wait("sync", "sDVE", sdm_free)
                sdma = b.dma("sync", "sW2", SDMB[:, 0:n * 512].rearrange("p (n q) -> p n q", q=512),
                             sdm_t[CUMPAD[p]:CUMPAD[p] + n].rearrange("n p q -> p n q"))
                for g in range(P4G):
                    gq = (p * 8 + g) % 2
                    b.wait("gpsimd", "sPE", kv_free[gq])
                    kvsem = "sX2" if gq == 0 else "sX3"
                    kall_t = b.dma("gpsimd", kvsem, KALL[gq][:, 0:n * 128],
                                   kspill[128 * g:128 * g + 128, 0:n * 128])
                    vall_t = b.dma("gpsimd", kvsem,
                                   VALL[gq][:, 0:n * 128].rearrange("p (n d) -> p n d", d=128),
                                   vspill[0:n * 128, 128 * g:128 * g + 128].rearrange("(n p) d -> p n d", p=128))
                    for pr in range(P4R):
                        h0 = 4 * g + 2 * pr
                        qsem = "sQa" if pr == 0 else "sQb"
                        b.wait("gpsimd", "sPE", qpr_free[pr])
                        b.dma("gpsimd", qsem, QPR[pr][:, 0:512],
                              qspill[128 * h0:128 * h0 + 128, 512 * p:512 * p + 512])
                        qpr_t = b.dma("gpsimd", qsem, QPR[pr][:, 512:1024],
                                      qspill[128 * h0 + 128:128 * h0 + 256, 512 * p:512 * p + 512])
                        st_done, exp_done, tmp_done = {}, {}, {}
                        for kt in range(n):
                            par = kt % 2
                            b.wait("vector", "sW2", sdma)
                            b.wait("vector", "sPE", st_done.get(kt - 2, 0))
                            b.op("vector", "tensor_scalar_mul", TMP[par][:, 0:512],
                                 SDMB[:, kt * 512:(kt + 1) * 512], -SLOPES[h0])
                            tmp_done[kt] = b.op("vector", "tensor_scalar_mul", TMP[par][:, 512:1024],
                                                SDMB[:, kt * 512:(kt + 1) * 512], -SLOPES[h0 + 1], sem="sDVE")
                            b.wait("tensor", kvsem, kall_t)
                            b.wait("tensor", qsem, qpr_t)
                            b.wait("tensor", "sDVE", tmp_done[kt])
                            b.wait("tensor", "sACT", exp_done.get(kt - 2, st_free[par]))
                            ps = psST[par]
                            b.mm(ps[:, 0:512], KALL[gq][:, kt * 128:(kt + 1) * 128],
                                 QPR[pr][:, 0:512], start=True, stop=False)
                            b.mm(ps[:, 0:512], idbf[:], TMP[par][:, 0:512], start=False, stop=True)
                            b.mm(ps[:, 512:1024], KALL[gq][:, kt * 128:(kt + 1) * 128],
                                 QPR[pr][:, 512:1024], start=True, stop=False)
                            st_done[kt] = b.mm(ps[:, 512:1024], idbf[:], TMP[par][:, 512:1024],
                                               start=False, stop=True, sem="sPE")
                            b.wait("scalar", "sPE", st_done[kt])
                            if kt == 0:
                                b.wait("scalar", "sPE", eall_free)
                            exp_done[kt] = b.op("scalar", "activation",
                                                EALL[:, kt * 1024:(kt + 1) * 1024], psST[par][:],
                                                AF.Exp, bias=-M_EXP, scale=1.0, sem="sACT")
                        if P4MODE == 1:
                            continue
                        b.wait("tensor", "sACT", max(exp_done[n - 1], pvO_free))
                        b.wait("tensor", kvsem, vall_t)
                        qpr_free[pr] = b.cnt["sPE"] + 4
                        for kt in range(n):
                            b.mm(ps0[:], VALL[gq][:, kt * 128:(kt + 1) * 128],
                                 EALL[:, kt * 1024:kt * 1024 + 512], start=(kt == 0), stop=(kt == n - 1),
                                 sem=("sPE" if kt == n - 1 else None))
                        for kt in range(n):
                            b.mm(ps1[:], VALL[gq][:, kt * 128:(kt + 1) * 128],
                                 EALL[:, kt * 1024 + 512:kt * 1024 + 1024], start=(kt == 0), stop=(kt == n - 1),
                                 sem=("sPE" if kt == n - 1 else None))
                        for kt in range(n):
                            b.mm(ps2[0:1, :], onec_bf[:], EALL[:, kt * 1024:kt * 1024 + 512],
                                 start=(kt == 0), stop=(kt == n - 1),
                                 sem=("sPE" if kt == n - 1 else None))
                        pv_all = None
                        for kt in range(n):
                            pv_all = b.mm(ps3[0:1, :], onec_bf[:], EALL[:, kt * 1024 + 512:kt * 1024 + 1024],
                                          start=(kt == 0), stop=(kt == n - 1),
                                          sem=("sPE" if kt == n - 1 else None))
                        eall_free = pv_all
                        if P4MODE == 2:
                            pvO_free = 0
                            continue
                        b.wait("scalar", "sPE", pv_all)
                        b.wait("scalar", "sDVE", ot_free)
                        b.op("scalar", "copy", r_sum, ps2[0:1, :])
                        b.op("scalar", "copy", r_sq, ps3[0:1, :])
                        b.op("scalar", "copy", OTb[0], ps0[:])
                        a_t1 = b.op("scalar", "copy", OTb[1], ps1[:], sem="sACT")
                        pvO_free = a_t1
                        st_free[1] = a_t1
                        if P4MODE == 3:
                            pvO_free = 0
                            st_free[1] = 0
                            continue
                        b.wait("vector", "sACT", a_t1)
                        b.op("vector", "reciprocal", r_m, r_sum)
                        b.op("vector", "reciprocal", r_t1, r_sq)
                        b.op("vector", "tensor_copy", rows_bf[0:1, 0:512], r_m)
                        b.op("vector", "tensor_sub", r_sum, r_m, rows_bf[0:1, 0:512])
                        b.op("vector", "tensor_copy", rows_bf[0:1, 512:1024], r_sum)
                        b.op("vector", "tensor_copy", rows_bf[0:1, 1024:1536], r_t1)
                        b.op("vector", "tensor_sub", r_sq, r_t1, rows_bf[0:1, 1024:1536])
                        v_t1 = b.op("vector", "tensor_copy", rows_bf[0:1, 1536:2048], r_sq, sem="sDVE")
                        b.wait("tensor", "sDVE", v_t1)
                        b.wait("tensor", "sACT", a_t1)
                        b.mm(psA[:, 0:512], onerb[:], rows_bf[0:1, 0:512], start=True, stop=False)
                        b.mm(psA[:, 0:512], onerb[:], rows_bf[0:1, 512:1024], start=False, stop=True)
                        b.mm(psA[:, 512:1024], onerb[:], rows_bf[0:1, 1024:1536], start=True, stop=False)
                        pe_t1 = b.mm(psA[:, 512:1024], onerb[:], rows_bf[0:1, 1536:2048], start=False, stop=True, sem="sPE")
                        if P4MODE == 4:
                            st_free[0] = 0
                            continue
                        b.wait("scalar", "sPE", pe_t1)
                        b.op("scalar", "copy", RTb[0], psA[:, 0:512])
                        a_t2 = b.op("scalar", "copy", RTb[1], psA[:, 512:1024], sem="sACT")
                        st_free[0] = a_t2
                        b.wait("vector", "sACT", a_t2)
                        b.wait("vector", "sXS", osb_free)
                        b.op("vector", "tensor_mul", OSB[:, h0 * 512:(h0 + 1) * 512], OTb[0], RTb[0])
                        ot_free = b.op("vector", "tensor_mul", OSB[:, (h0 + 1) * 512:(h0 + 2) * 512],
                                       OTb[1], RTb[1], sem="sDVE")
                    kv_free[gq] = b.cnt["sPE"]
                sdm_free = b.cnt["sDVE"]
                b.wait("gpsimd", "sDVE", b.cnt["sDVE"])
                osb_free = b.dma("gpsimd", "sXS",
                                 ospill[:, 512 * p:512 * p + 512].rearrange("(hq d) q -> d hq q", d=128),
                                 OSB.rearrange("d (hq q) -> d hq q", q=512))
            b.barrier()

        if upto >= 5:
            cv = Carve(pool)
            QGsb = [cv.take(1024, I32) for _ in range(2)]
            QUsb = [cv.take(1024, I32) for _ in range(2)]
            QDsb = [cv.take(512, I32) for _ in range(2)]
            DU = [cv.take(4096, U8) for _ in range(2)]
            SBT = [cv.take(4096, BF16) for _ in range(2)]
            WPT = [cv.take(4096, BF16) for _ in range(2)]
            SDT = [cv.take(4096, BF16) for _ in range(2)]
            WDP = [cv.take(4096, BF16) for _ in range(2)]

            qg_free = [0, 0]
            sbt_free = [0, 0]
            wpt_free = [0, 0]
            bi = 0
            for j in range(43):
                pj = j % 2
                b.wait("sync", "sDVE", qg_free[pj])
                qg_tgt = b.dma("sync", "sW2", QGsb[pj], qwg_t[j])
                qu_tgt = b.dma("sync", "sW2", QUsb[pj], qwu_t[j])
                for plane in (0, 1):
                    m = plane * 43 + j
                    for QS, sdram, wsp, qtgt in ((QGsb[pj], sgBbf, wgsp, qg_tgt),
                                                 (QUsb[pj], suBbf, wusp, qu_tgt)):
                        par = bi % 2
                        bi += 1
                        ssem = "sWa" if par == 0 else "sWb"
                        b.wait("sync", "sDVE", sbt_free[par])
                        sb_t = 0
                        for hf in (0, 1):
                            sb_t = b.dma("sync", ssem, SBT[par][64 * hf:64 * hf + 64, :],
                                         sdram[m, hf:hf + 1, :].broadcast_to([64, 4096]))
                        b.wait("vector", "sW2", qtgt)
                        b.wait("vector", ssem, sb_t)
                        b.wait("vector", "sSa" if par == 0 else "sSb", wpt_free[par])
                        qv8 = QS.bitcast(U8)
                        if plane == 0:
                            b.op("vector", "tensor_single_scalar", DU[par], qv8, 15, ALU.bitwise_and)
                        else:
                            b.op("vector", "tensor_single_scalar", DU[par], qv8, 4, ALU.logical_shift_right)
                        qg_free[pj] = b.cnt["sDVE"] + 1
                        sbt_free[par] = b.cnt["sDVE"] + 1
                        vmul = b.op("vector", "tensor_tensor", WPT[par], DU[par], SBT[par],
                                    ALU.mult, sem="sDVE")
                        b.wait("gpsimd", "sDVE", vmul)
                        wpt_free[par] = b.dma("gpsimd", "sSa" if par == 0 else "sSb",
                                              wsp[m], WPT[par])
            qd_free = [0, 0]
            sdt_free = [0, 0]
            wdp_free = [0, 0]
            for m in range(NI):
                pm = m % 2
                j = m % 43
                b.wait("sync", "sDVE", qd_free[pm])
                qd_t = b.dma("sync", "sW2", QDsb[pm], qwd_t[m])
                dsem = "sQa" if pm == 0 else "sQb"
                b.wait("sync", "sDVE", sdt_free[pm])
                sd_t = 0
                for hf in (0, 1):
                    dst = SDT[pm][64 * hf:64 * hf + 64, :]
                    src = sdbf[2 * j + hf:2 * j + hf + 1, :].broadcast_to([64, H])
                    sd_t = b.dma("sync", dsem, dst, src)
                b.wait("vector", "sW2", qd_t)
                b.wait("vector", dsem, sd_t)
                b.wait("vector", "sSa" if pm == 0 else "sSb", wdp_free[pm])
                qv8 = QDsb[pm].bitcast(U8)
                b.op("vector", "tensor_single_scalar", DU[pm][:, 0:2048], qv8, 15, ALU.bitwise_and)
                b.op("vector", "tensor_single_scalar", DU[pm][:, 2048:4096], qv8, 4, ALU.logical_shift_right)
                qd_free[pm] = b.cnt["sDVE"] + 1
                sdt_free[pm] = b.cnt["sDVE"] + 1
                vmul = b.op("vector", "tensor_tensor", WDP[pm], DU[pm], SDT[pm], ALU.mult, sem="sDVE")
                b.wait("gpsimd", "sDVE", vmul)
                last = 0
                for hc in range(32):
                    last = b.dma("gpsimd", "sSa" if pm == 0 else "sSb",
                                 wdsp[hc, :, 128 * m:128 * m + 128],
                                 WDP[pm][:, 128 * hc:128 * hc + 128])
                wdp_free[pm] = last
            b.barrier()

        if upto >= 6:
            cv = Carve(pool)
            OSB2 = cv.take(NF * 512, BF16)
            WST2 = cv.take(NF * 128, F32)
            WBF2 = [cv.take(NF * 128, BF16) for _ in range(2)]
            OUT1 = cv.take(NF * 512, F32)
            XRS = [cv.take(512, F32) for _ in range(2)]
            O1T = [cv.take(512, F32) for _ in range(2)]
            O1B = [cv.take(512, BF16) for _ in range(2)]
            SQ2 = [cv.take(512, BF16) for _ in range(2)]
            XL2 = [cv.take(512, BF16) for _ in range(2)]
            RTMT2 = cv.take(1024, F32)
            RT2 = RTMT2[:, 0:512]
            MT2 = RTMT2[:, 512:1024]
            TT2 = [cv.take(512, F32) for _ in range(2)]

            wcast_last = 0
            pe_w = [0, 0]
            evac_done = [0, 0]
            stat_pe = [0, 0]
            xr_free = [0, 0]
            rows_free = 0
            rtmt_guard2 = 0
            last_apply2 = 0
            xl2_dma = [0, 0]
            out1_dma_last = 0
            psK = [ps0, ps1]
            for t in range(4):
                ts0 = 512 * t
                b.wait("gpsimd", "sPE", b.cnt["sPE"])
                b.wait("vector", "sXS", out1_dma_last)
                odma = b.dma("gpsimd", "sX", OSB2.rearrange("p (f q) -> p f q", q=512),
                             ospill[:, ts0:ts0 + 512].rearrange("(f p) q -> p f q", p=128))
                pe_stats = 0
                for oc in range(NF):
                    par = oc % 2
                    b.wait("sync", "sDVE", wcast_last)
                    tw = b.dma("sync", "sW", WST2.rearrange("p (f c) -> p f c", c=128), wo_t[oc].rearrange("f p c -> p f c"))
                    b.wait("vector", "sW", tw)
                    b.wait("vector", "sPE", pe_w[par])
                    wcast_last = b.op("vector", "tensor_copy", WBF2[par], WST2, sem="sDVE")
                    b.wait("gpsimd", "sDVE", xr_free[par])
                    xr = b.dma("gpsimd", "sQa" if par == 0 else "sQb",
                               XRS[par], x_myT[128 * oc:128 * oc + 128, ts0:ts0 + 512])
                    b.wait("tensor", "sDVE", wcast_last)
                    b.wait("tensor", "sX", odma)
                    b.wait("tensor", "sACT", evac_done[par])
                    pe = None
                    for f in range(NF):
                        last = (f == NF - 1)
                        pe = b.mm(psK[par][:], WBF2[par][:, f * 128:(f + 1) * 128],
                                  OSB2[:, f * 512:(f + 1) * 512],
                                  start=(f == 0), stop=last, sem=("sPE" if last else None))
                    pe_w[par] = pe
                    b.wait("scalar", "sPE", pe)
                    evac_done[par] = b.op("scalar", "copy", O1T[par], psK[par][:], sem="sACT")
                    b.wait("vector", "sACT", evac_done[par])
                    b.wait("vector", "sQa" if par == 0 else "sQb", xr)
                    b.wait("vector", "sPE", stat_pe[par])
                    b.op("vector", "tensor_add", OUT1[:, oc * 512:(oc + 1) * 512], O1T[par], XRS[par])
                    xr_free[par] = b.cnt["sDVE"] + 1
                    v1 = b.op("vector", "tensor_copy", O1B[par], OUT1[:, oc * 512:(oc + 1) * 512], sem="sDVE")
                    b.wait("gpsimd", "sDVE", v1)
                    out1_dma_last = b.dma("gpsimd", "sXS", out1sp[128 * oc:128 * oc + 128, ts0:ts0 + 512],
                                          OUT1[:, oc * 512:(oc + 1) * 512])
                    b.wait("scalar", "sDVE", v1)
                    aq = b.op("scalar", "activation", SQ2[par], O1B[par], AF.Square, sem="sACT")
                    b.wait("tensor", "sACT", aq)
                    if oc == 0:
                        b.wait("tensor", "sACT", rows_free)
                    b.mm(ps2[0:1, :], onec_bf[:], O1B[par], start=(oc == 0), stop=(oc == NF - 1))
                    pe_stats = stat_pe[par] = b.mm(ps3[0:1, :], onec_bf[:], SQ2[par],
                                                   start=(oc == 0), stop=(oc == NF - 1), sem="sPE")
                t_rt2, rows_free = ln_rows(pe_stats, RTMT2, rtmt_guard2,
                                           apply_guard=last_apply2)
                rtmt_guard2 = t_rt2
                for oc in range(NF):
                    par = oc % 2
                    b.wait("vector", "sACT", t_rt2)
                    b.wait("vector", "sSa" if par == 0 else "sSb", xl2_dma[par])
                    b.op("vector", "tensor_mul", TT2[par], OUT1[:, oc * 512:(oc + 1) * 512], RT2)
                    v2 = b.op("vector", "tensor_sub", XL2[par], TT2[par], MT2, sem="sDVE")
                    last_apply2 = v2
                    b.wait("gpsimd", "sDVE", v2)
                    xl2_dma[par] = b.dma("gpsimd", "sSa" if par == 0 else "sSb",
                                         xln2sp[128 * oc:128 * oc + 128, ts0:ts0 + 512], XL2[par])
            b.barrier()

        if upto >= 7:
            cv = Carve(pool)
            XLN2 = cv.take(NF * 2048, BF16)
            WG = [cv.take(4096, BF16) for _ in range(2)]
            WU = [cv.take(4096, BF16) for _ in range(2)]
            HB = cv.take(2048, BF16)
            SGB = [cv.take(512, BF16) for _ in range(2)]
            UPB = [cv.take(512, BF16) for _ in range(2)]

            for fc in range(NF):
                b.dma("gpsimd", "sX", XLN2[:, fc * 2048:(fc + 1) * 2048],
                      xln2sp[128 * fc:128 * fc + 128, :])
            xl_tgt = b.cnt["sX"]
            xs_guard = 0
            for t in range(4):
                b.wait("tensor", "sX", xl_tgt)
                b.wait("tensor", "sACT", xs_guard)
                pe = None
                for fc in range(NF):
                    last = (fc == NF - 1)
                    pe = b.mm(ps2[0:32, :], b2sb[:, (fc % 16) * 32:(fc % 16) * 32 + 32],
                              XLN2[:, fc * 2048 + 512 * t: fc * 2048 + 512 * t + 512],
                              start=(fc == 0), stop=last, sem=("sPE" if last else None))
                b.wait("scalar", "sPE", pe)
                xs_guard = b.op("scalar", "copy", xs2sb[:, 512 * t:512 * t + 512], ps2[0:32, :], sem="sACT")
            a_xs2 = xs_guard
            if dbg:
                b.wait("gpsimd", "sACT", a_xs2)
                b.dma("gpsimd", "sX", xs2dump, xs2sb[:])

            w_free = [0, 0]
            gu_free = [[0, 0], [0, 0]]
            hb_dma = 0
            psG = [ps0, ps1]
            for m in range(NI):
                pm = m % 2
                wsem = "sWa" if pm == 0 else "sWb"
                b.wait("sync", "sPE", w_free[pm])
                wgt = b.dma("sync", wsem, WG[pm], wgsp[m])
                wut = b.dma("sync", wsem, WU[pm], wusp[m])
                vg_t = b.dma("sync", wsem, vgt[:, pm * 128:pm * 128 + 128], vg_d[:, 128 * m:128 * m + 128])
                vu_t = b.dma("sync", wsem, vut[:, pm * 128:pm * 128 + 128], vu_d[:, 128 * m:128 * m + 128])
                pe_last = None
                for t in range(4):
                    pt = t % 2
                    b.wait("tensor", wsem, max(wgt, vg_t))
                    b.wait("tensor", "sACT", max(a_xs2, gu_free[0][pt]))
                    for f in range(NF):
                        b.mm(psG[pt][:], WG[pm][:, f * 128:(f + 1) * 128],
                             XLN2[:, f * 2048 + 512 * t: f * 2048 + 512 * t + 512],
                             start=(f == 0), stop=False)
                    peg = b.mm(psG[pt][:], vgt[0:32, pm * 128:pm * 128 + 128], xs2sb[:, 512 * t:512 * t + 512],
                               start=False, stop=True, sem="sPE")
                    b.wait("tensor", wsem, max(wut, vu_t))
                    b.wait("tensor", "sACT", gu_free[1][pt])
                    for f in range(NF):
                        b.mm(psA[:, pt * 512:pt * 512 + 512], WU[pm][:, f * 128:(f + 1) * 128],
                             XLN2[:, f * 2048 + 512 * t: f * 2048 + 512 * t + 512],
                             start=(f == 0), stop=False)
                    peu = b.mm(psA[:, pt * 512:pt * 512 + 512], vut[0:32, pm * 128:pm * 128 + 128],
                               xs2sb[:, 512 * t:512 * t + 512], start=False, stop=True, sem="sPE")
                    pe_last = peu
                    b.wait("scalar", "sPE", peu)
                    b.op("scalar", "activation", SGB[pt], psG[pt][:], AF.Silu, bias=0.0, scale=1.0)
                    gu_free[0][pt] = gu_free[1][pt] = aev = b.op(
                        "scalar", "copy", UPB[pt], psA[:, pt * 512:pt * 512 + 512], sem="sACT")
                    if dbg and m == 0:
                        b.wait("gpsimd", "sACT", aev)
                        b.dma("gpsimd", "sX", hdump[:, 1024 * t:1024 * t + 512], SGB[pt])
                        b.dma("gpsimd", "sX", hdump[:, 1024 * t + 512:1024 * t + 1024], UPB[pt])
                        b.wait("vector", "sX", b.cnt["sX"])
                    b.wait("vector", "sACT", aev)
                    b.wait("vector", "sXS", hb_dma)
                    b.op("vector", "tensor_mul", HB[:, 512 * t:512 * t + 512], SGB[pt], UPB[pt],
                         sem="sDVE")
                w_free[pm] = pe_last
                b.wait("gpsimd", "sDVE", b.cnt["sDVE"])
                hb_dma = b.dma("gpsimd", "sXS", hspill[128 * m:128 * m + 128, :], HB)
            b.barrier()

        if upto >= 8:
            cv = Carve(pool)
            HSL = cv.take(NI * 512, BF16)
            WDC = [cv.take(NI * 128, BF16) for _ in range(2)]
            OU1 = [cv.take(512, F32) for _ in range(2)]
            DTb = [cv.take(512, F32) for _ in range(2)]
            FBb = [cv.take(512, F32) for _ in range(2)]

            wd_free = [0, 0]
            dt_free = [0, 0]
            fb_dma = [0, 0]
            ou_free = [0, 0]
            hs_guard = 0
            psD = [ps0, ps1]
            for t in range(4):
                ts0 = 512 * t
                b.wait("gpsimd", "sPE", b.cnt["sPE"])
                hd = b.dma("gpsimd", "sX", HSL.rearrange("p (m q) -> p m q", q=512),
                           hspill[:, ts0:ts0 + 512].rearrange("(m p) q -> p m q", p=128))
                b.wait("tensor", "sX", hd)
                b.wait("tensor", "sACT", hs_guard)
                pe = None
                for m in range(NI):
                    last = (m == NI - 1)
                    pe = b.mm(psA[0:86, 0:512], bdsb[:, (m % 43) * 86:(m % 43) * 86 + 86],
                              HSL[:, m * 512:(m + 1) * 512],
                              start=(m == 0), stop=last, sem=("sPE" if last else None))
                b.wait("scalar", "sPE", pe)
                hs_guard = b.op("scalar", "copy", hssb[:], psA[0:86, 0:512], sem="sACT")
                a_hs = hs_guard
                for hc in range(32):
                    ph = hc % 2
                    wsem8 = "sWa" if ph == 0 else "sWb"
                    b.wait("sync", "sPE", wd_free[ph])
                    wd_t = b.dma("sync", wsem8, WDC[ph], wdsp[hc])
                    vd_t = b.dma("sync", wsem8, vdt[:, ph * 128:ph * 128 + 128],
                                 vd_d[:, 128 * hc:128 * hc + 128])
                    b.wait("gpsimd", "sDVE", ou_free[ph])
                    ou_t = b.dma("gpsimd", "sQa" if ph == 0 else "sQb",
                                 OU1[ph], out1sp[128 * hc:128 * hc + 128, ts0:ts0 + 512])
                    b.wait("tensor", wsem8, max(wd_t, vd_t))
                    b.wait("tensor", "sACT", max(a_hs, dt_free[ph]))
                    for m in range(NI):
                        b.mm(psD[ph][:], WDC[ph][:, m * 128:(m + 1) * 128],
                             HSL[:, m * 512:(m + 1) * 512], start=(m == 0), stop=False)
                    pe_d = b.mm(psD[ph][:], vdt[0:86, ph * 128:ph * 128 + 128], hssb[:],
                                start=False, stop=True, sem="sPE")
                    wd_free[ph] = pe_d
                    b.wait("scalar", "sPE", pe_d)
                    dt_free[ph] = b.op("scalar", "copy", DTb[ph], psD[ph][:], sem="sACT")
                    b.wait("vector", "sACT", dt_free[ph])
                    b.wait("vector", "sQa" if ph == 0 else "sQb", ou_t)
                    b.wait("vector", "sSa" if ph == 0 else "sSb", fb_dma[ph])
                    vf = b.op("vector", "tensor_add", FBb[ph], DTb[ph], OU1[ph], sem="sDVE")
                    ou_free[ph] = vf
                    b.wait("gpsimd", "sDVE", vf)
                    fb_dma[ph] = b.dma("gpsimd", "sSa" if ph == 0 else "sSb",
                                       outT[128 * hc:128 * hc + 128, ts0:ts0 + 512], FBb[ph])

        b.wait("gpsimd", "sX", b.cnt["sX"])
        b.wait("gpsimd", "sX2", b.cnt["sX2"])
        b.wait("gpsimd", "sX3", b.cnt["sX3"])
        b.wait("gpsimd", "sXS", b.cnt["sXS"])
        b.wait("gpsimd", "sX4", b.cnt["sX4"])
        b.wait("gpsimd", "sWa", b.cnt["sWa"])
        b.wait("gpsimd", "sWb", b.cnt["sWb"])
        b.wait("gpsimd", "sQa", b.cnt["sQa"])
        b.wait("gpsimd", "sQb", b.cnt["sQb"])
        b.wait("gpsimd", "sSa", b.cnt["sSa"])
        b.wait("gpsimd", "sSb", b.cnt["sSb"])

        @block.sync
        def _(e):
            for f in b.streams["sync"]:
                f(e)

        @block.vector
        def _(e):
            for f in b.streams["vector"]:
                f(e)

        @block.scalar
        def _(e):
            for f in b.streams["scalar"]:
                f(e)

        @block.tensor
        def _(e):
            for f in b.streams["tensor"]:
                f(e)

        @block.gpsimd
        def _(e):
            for f in b.streams["gpsimd"]:
                f(e)

    return nc


def _pack_shared(wqkv, wo, gqw, gsc, gqz, uqw, usc, uqz, dqw, dsc, dqz):
    wqkv_p = np.ascontiguousarray(wqkv[PERM_H, :])
    wqkv_t = np.ascontiguousarray(
        wqkv_p[:, :5120].reshape(32, 128, 40, 128).transpose(2, 0, 1, 3))
    wv_t = np.ascontiguousarray(wqkv_p[:, 5120:].reshape(32, 128, 1024))
    wo_t = np.ascontiguousarray(wo[:, PERM_H].reshape(32, 128, 32, 128).transpose(2, 0, 1, 3))
    qwg_t = np.ascontiguousarray(
        gqw[PERM_H, :].reshape(32, 128, 43, 32).transpose(2, 1, 0, 3).reshape(43, 128, 1024))
    qwu_t = np.ascontiguousarray(
        uqw[PERM_H, :].reshape(32, 128, 43, 32).transpose(2, 1, 0, 3).reshape(43, 128, 1024))
    qwd_t = np.ascontiguousarray(dqw[PERM_I, :].reshape(86, 128, 512))
    sg_dev = np.ascontiguousarray(gsc[:, PERM_I])
    su_dev = np.ascontiguousarray(usc[:, PERM_I])
    sd_dev = np.ascontiguousarray(dsc[:, PERM_H])

    cidx = np.arange(4096)
    grow = 2 * ((cidx // 128) % 16)
    tcol = cidx % 128
    sgB = np.empty((86, 2, 4096), dtype=np.float32)
    suB = np.empty((86, 2, 4096), dtype=np.float32)
    for m in range(86):
        for hf in range(2):
            sgB[m, hf] = sg_dev[grow + hf, 128 * m + tcol]
            suB[m, hf] = su_dev[grow + hf, 128 * m + tcol]

    p = np.arange(128)
    b2 = np.zeros((128, 512), dtype=np.float32)
    for j in range(16):
        b2[p, j * 32 + 2 * j + (p >= 64)] = 1.0
    bd = np.zeros((128, 43 * 86), dtype=np.float32)
    for j in range(43):
        bd[p, j * 86 + 2 * j + (p >= 64)] = 1.0
    cstv = np.zeros((1, 16), dtype=np.float32)
    cstv[0, :len(CONSTS)] = CONSTS

    return {
        "wqkv_t": wqkv_t, "wv_t": wv_t, "wo_t": wo_t,
        "qwg_t": qwg_t, "qwu_t": qwu_t, "qwd_t": qwd_t,
        "sg_dev": sg_dev, "su_dev": su_dev, "sd_dev": sd_dev,
        "sgB_t": sgB, "suB_t": suB,
        "qzg": np.ascontiguousarray(gqz), "qzu": np.ascontiguousarray(uqz),
        "qzd": np.ascontiguousarray(dqz),
        "b2_in": b2.astype(BF), "bd_in": bd.astype(BF),
        "id_in": np.eye(128, dtype=np.float32).astype(BF),
        "cst_in": cstv,
    }


def _sdm_table(half):
    out = np.empty((NKT, 128, 512), dtype=np.float32)
    for pos, blk in enumerate(MYBLOCKS[half]):
        q0 = 512 * blk
        for kt in range(PADDED[pos]):
            k = 128 * kt + np.arange(128)[:, None]
            q = q0 + np.arange(512)[None, :]
            rel = q - k
            v = np.where(rel >= 0, np.sqrt(np.maximum(rel, 0).astype(np.float64)), 1e30)
            out[CUMPAD[pos] + kt] = v.astype(np.float32)
    return out.astype(BF)


def make_inmaps(x, wqkv, wo, gqw, gsc, gqz, uqw, usc, uqz, dqw, dsc, dqz, ncore=8):
    shared = _pack_shared(wqkv, wo, gqw, gsc, gqz, uqw, usc, uqz, dqw, dsc, dqz)
    sdm = [_sdm_table(h) for h in range(2)]
    in_maps = []
    my_toks = []
    for c in range(ncore):
        bb, hh = c // 2, c % 2
        xT = x[bb].T[PERM_H, :]
        my_tok = np.concatenate([np.arange(512 * blk, 512 * blk + 512) for blk in MYBLOCKS[hh]])
        my_toks.append(my_tok)
        m = dict(shared)
        m["x_batchT"] = np.ascontiguousarray(xT.astype(np.float32))
        m["x_myT"] = np.ascontiguousarray(xT[:, my_tok].astype(np.float32))
        m["sdm_t"] = sdm[hh]
        in_maps.append(m)
    return in_maps, my_toks


def kernel(L0_attn_hidden_states, L0_attn_w_qkv, L0_attn_w_o,
           L0_ffn_gate_qweight, L0_ffn_gate_scales, L0_ffn_gate_qzeros,
           L0_ffn_up_qweight, L0_ffn_up_scales, L0_ffn_up_qzeros,
           L0_ffn_down_qweight, L0_ffn_down_scales, L0_ffn_down_qzeros):
    in_maps, my_toks = make_inmaps(
        np.asarray(L0_attn_hidden_states),
        np.asarray(L0_attn_w_qkv), np.asarray(L0_attn_w_o),
        np.asarray(L0_ffn_gate_qweight), np.asarray(L0_ffn_gate_scales),
        np.asarray(L0_ffn_gate_qzeros),
        np.asarray(L0_ffn_up_qweight), np.asarray(L0_ffn_up_scales),
        np.asarray(L0_ffn_up_qzeros),
        np.asarray(L0_ffn_down_qweight), np.asarray(L0_ffn_down_scales),
        np.asarray(L0_ffn_down_qzeros))

    nc = build_program(upto=8)
    res = run_bass_kernel_spmd(nc, in_maps, core_ids=list(range(8)))

    out = np.empty((B, S, H), dtype=np.float32)
    for c in range(8):
        bb = c // 2
        o = np.asarray(res.results[c]["outT"])
        tmp = np.empty((H, TOK), dtype=np.float32)
        tmp[PERM_H, :] = o
        out[bb, my_toks[c], :] = tmp.T
    return out


# version 03
# speedup vs baseline: 1.0018x; 1.0018x over previous
import numpy as np
import ml_dtypes
import concourse.bass as bass
import concourse.mybir as mybir
from concourse.bass_utils import run_bass_kernel_spmd

F32 = mybir.dt.float32
BF16 = mybir.dt.bfloat16
I32 = mybir.dt.int32
U8 = mybir.dt.uint8
AF = mybir.ActivationFunctionType
ALU = mybir.AluOpType
BF = ml_dtypes.bfloat16

B, S, H, NH, NKV, D, I = 4, 4096, 4096, 32, 8, 128, 11008
TOK = 2048
NF = H // 128
NI = I // 128
M_EXP = 30.0
QK_SCALE = D ** -0.5
SLOPES = [2.0 ** (-0.25 * (hq + 1)) for hq in range(NH)]
MYBLOCKS = [[0, 3, 4, 7], [1, 2, 5, 6]]
PADDED = [8, 16, 24, 32]
CUMPAD = [0, 8, 24, 48]
NKT = 80
PAT = [[0, 4, 1, 5], [2, 6, 3, 7]]
EPS = 1e-5
POOL_LIM = 180224

ENGINES = ("sync", "vector", "scalar", "tensor", "gpsimd")


def _perm(n_j):
    n = n_j * 256
    p = np.empty(n, dtype=np.int64)
    for plane in range(2):
        for j in range(n_j):
            t = np.arange(128)
            p[plane * n_j * 128 + 128 * j + t] = 8 * (32 * j + t // 4) + np.array(PAT[plane])[t % 4]
    return p


PERM_H = _perm(16)
PERM_I = _perm(43)
CONSTS = [-M_EXP, EPS, -0.5, 1.0 / 4096.0, QK_SCALE, 1.0, 0.0, -1.0]


class Bld:
    def __init__(self, nc, sems):
        self.nc = nc
        self.sems = sems
        self.streams = {e: [] for e in ENGINES}
        self.cnt = {k: 0 for k in sems}

    def raw(self, eng, fn):
        self.streams[eng].append(fn)

    def wait(self, eng, sem, val):
        if val is None or val <= 0:
            return
        s = self.sems[sem]
        self.raw(eng, lambda e, s=s, v=val: e.wait_ge(s, v))

    def dma(self, eng, sem, out, in_):
        s = self.sems[sem]
        self.cnt[sem] += 16
        tgt = self.cnt[sem]
        self.raw(eng, lambda e, o=out, i=in_, s=s: e.dma_start(out=o, in_=i).then_inc(s, 16))
        return tgt

    def op(self, eng, meth, *a, sem=None, **kw):
        s = tgt = None
        if sem is not None:
            s = self.sems[sem]
            self.cnt[sem] += 1
            tgt = self.cnt[sem]

        def f(e, meth=meth, a=a, kw=kw, s=s):
            inst = getattr(e, meth)(*a, **kw)
            if s is not None:
                inst.then_inc(s, 1)
        self.raw(eng, f)
        return tgt

    def mm(self, out, lhsT, rhs, start, stop, sem=None):
        return self.op("tensor", "matmul", out, lhsT, rhs, start=start, stop=stop, sem=sem)

    def barrier(self):
        snap = dict(self.cnt)
        for e in ENGINES:
            for k, v in snap.items():
                self.wait(e, k, v)


def ceil4(x):
    return (x + 3) & ~3


class Carve:
    def __init__(self, pool):
        self.pool = pool
        self.off = 0

    def take(self, nelem, dtype, rows=128):
        size = {F32: 4, BF16: 2, I32: 4, U8: 1}[dtype]
        nbytes = ceil4(nelem * size)
        assert self.off + nbytes <= POOL_LIM, f"pool overflow {self.off + nbytes}"
        v = self.pool[0:rows, self.off:self.off + nbytes]
        if dtype != U8:
            v = v.bitcast(dtype)
        self.off += nbytes
        return v


def build_program(upto=8, dbg=False):
    nc = bass.Bass("TRN2", target_bir_lowering=False)
    dt = nc.dram_tensor
    DK = "ExternalOutput" if dbg else "Internal"

    x_batchT = dt("x_batchT", [H, S], F32, kind="ExternalInput").ap()
    x_myT = dt("x_myT", [H, TOK], F32, kind="ExternalInput").ap()
    wqkv_t = dt("wqkv_t", [40, 32, 128, 128], F32, kind="ExternalInput").ap()
    wv_t = dt("wv_t", [32, 128, 1024], F32, kind="ExternalInput").ap()
    wo_t = dt("wo_t", [32, 32, 128, 128], F32, kind="ExternalInput").ap()
    qwg_t = dt("qwg_t", [43, 128, 1024], I32, kind="ExternalInput").ap()
    qwu_t = dt("qwu_t", [43, 128, 1024], I32, kind="ExternalInput").ap()
    qwd_t = dt("qwd_t", [86, 128, 512], I32, kind="ExternalInput").ap()
    sg_dev = dt("sg_dev", [32, I], F32, kind="ExternalInput").ap()
    su_dev = dt("su_dev", [32, I], F32, kind="ExternalInput").ap()
    sgB_t = dt("sgB_t", [86, 2, 4096], F32, kind="ExternalInput").ap()
    suB_t = dt("suB_t", [86, 2, 4096], F32, kind="ExternalInput").ap()
    sd_dev = dt("sd_dev", [86, H], F32, kind="ExternalInput").ap()
    qzg = dt("qzg", [32, 1376], I32, kind="ExternalInput").ap()
    qzu = dt("qzu", [32, 1376], I32, kind="ExternalInput").ap()
    qzd = dt("qzd", [86, 512], I32, kind="ExternalInput").ap()
    sdm_t = dt("sdm_t", [NKT, 128, 512], BF16, kind="ExternalInput").ap()
    b2_in = dt("b2_in", [128, 512], BF16, kind="ExternalInput").ap()
    bd_in = dt("bd_in", [128, 43 * 86], BF16, kind="ExternalInput").ap()
    id_in = dt("id_in", [128, 128], BF16, kind="ExternalInput").ap()
    cst_in = dt("cst_in", [1, 16], F32, kind="ExternalInput").ap()

    outT = dt("outT", [H, TOK], F32, kind="ExternalOutput").ap()

    kspill = dt("kspill", [NKV * D, S], BF16, kind=DK).ap()
    vspill = dt("vspill", [S, NKV * D], BF16, kind=DK).ap()
    qspill = dt("qspill", [NH * D, TOK], BF16, kind=DK).ap()
    ospill = dt("ospill", [NH * D, TOK], BF16, kind=DK).ap()
    xln1sp = dt("xln1sp", [H, S], BF16, kind=DK).ap()
    out1sp = dt("out1sp", [H, TOK], F32, kind=DK).ap()
    xln2sp = dt("xln2sp", [H, TOK], BF16, kind=DK).ap()
    hspill = dt("hspill", [I, TOK], BF16, kind=DK).ap()
    wgsp = dt("wgsp", [NI, 128, 4096], BF16, kind=DK).ap()
    wusp = dt("wusp", [NI, 128, 4096], BF16, kind=DK).ap()
    wdsp = dt("wdsp", [32, 128, NI * 128], BF16, kind=DK).ap()
    sgbf = dt("sgbf", [32, I], BF16, kind="Internal").ap()
    subf = dt("subf", [32, I], BF16, kind="Internal").ap()
    sgBbf = dt("sgBbf", [86, 2, 4096], BF16, kind="Internal").ap()
    wobf_d = dt("wobf_d", [32, 128, 4096], BF16, kind="Internal").ap()
    suBbf = dt("suBbf", [86, 2, 4096], BF16, kind="Internal").ap()
    sdbf = dt("sdbf", [86, H], BF16, kind="Internal").ap()
    rowsp = dt("rowsp", [1, 1024], F32, kind="Internal").ap()
    xs2dump = dt("xs2dump", [32, TOK], BF16, kind=DK).ap()
    hdump = dt("hdump", [128, 4096], BF16, kind=DK).ap()
    vg_d = dt("vg_d", [32, I], BF16, kind=DK).ap()
    vu_d = dt("vu_d", [32, I], BF16, kind=DK).ap()
    vd_d = dt("vd_d", [86, H], BF16, kind=DK).ap()

    from contextlib import ExitStack
    with ExitStack() as stack:
        ec = stack.enter_context
        cst = ec(nc.sbuf_tensor("cst", [128, 16], F32))
        idbf = ec(nc.sbuf_tensor("idbf", [128, 128], BF16))
        onec_bf = ec(nc.sbuf_tensor("onec_bf", [128, 1], BF16))
        oner_f = ec(nc.sbuf_tensor("oner_f", [1, 128], F32))
        onerb = ec(nc.sbuf_tensor("onerb", [1, 128], BF16))
        rows_bf = ec(nc.sbuf_tensor("rows_bf", [1, 2048], BF16))
        rows = ec(nc.sbuf_tensor("rows", [1, 3072], F32))
        b2sb = ec(nc.sbuf_tensor("b2sb", [128, 512], BF16))
        bdsb = ec(nc.sbuf_tensor("bdsb", [128, 43 * 86], BF16))
        xs2sb = ec(nc.sbuf_tensor("xs2sb", [32, TOK], BF16))
        hssb = ec(nc.sbuf_tensor("hssb", [86, 512], BF16))
        vgt = ec(nc.sbuf_tensor("vgt", [32, 256], BF16))
        vut = ec(nc.sbuf_tensor("vut", [32, 256], BF16))
        vdt = ec(nc.sbuf_tensor("vdt", [86, 256], BF16))
        pool = ec(nc.sbuf_tensor("pool", [128, POOL_LIM], U8))
        psA = ec(nc.psum_tensor("psA", [128, 1024], F32))
        psB = ec(nc.psum_tensor("psB", [128, 1024], F32))
        ps0 = ec(nc.psum_tensor("ps0", [128, 512], F32))
        ps1 = ec(nc.psum_tensor("ps1", [128, 512], F32))
        ps2 = ec(nc.psum_tensor("ps2", [128, 512], F32))
        ps3 = ec(nc.psum_tensor("ps3", [128, 512], F32))
        _sW = ec(nc.semaphore("sW"))
        _sW2 = ec(nc.semaphore("sW2"))
        _sX = ec(nc.semaphore("sX"))
        _sX2 = ec(nc.semaphore("sX2"))
        _sX3 = ec(nc.semaphore("sX3"))
        _sX4 = ec(nc.semaphore("sX4"))
        _sXS = ec(nc.semaphore("sXS"))
        _sWa = ec(nc.semaphore("sWa"))
        _sWb = ec(nc.semaphore("sWb"))
        _sQa = ec(nc.semaphore("sQa"))
        _sQb = ec(nc.semaphore("sQb"))
        _sSa = ec(nc.semaphore("sSa"))
        _sSb = ec(nc.semaphore("sSb"))
        _sPE = ec(nc.semaphore("sPE"))
        _sACT = ec(nc.semaphore("sACT"))
        _sDVE = ec(nc.semaphore("sDVE"))
        block = ec(nc.Block())
        for ci, v in enumerate(CONSTS):
            nc.const_aps.register(F32, v, cst[:, ci:ci + 1])

        sems = {"sW": _sW, "sW2": _sW2, "sX": _sX, "sX2": _sX2, "sX3": _sX3,
                "sX4": _sX4, "sXS": _sXS, "sWa": _sWa, "sWb": _sWb,
                "sQa": _sQa, "sQb": _sQb, "sSa": _sSa, "sSb": _sSb,
                "sPE": _sPE, "sACT": _sACT, "sDVE": _sDVE}
        b = Bld(nc, sems)

        def row(i):
            return rows[0:1, i * 512:(i + 1) * 512]
        r_sum, r_sq, r_m, r_t1, r_rstd, r_mr = (row(i) for i in range(6))

        b.dma("sync", "sW", cst[:], cst_in.broadcast_to([128, 16]))
        b.dma("sync", "sW", idbf[:], id_in)
        b.dma("sync", "sW", b2sb[:], b2_in)
        t0 = b.dma("sync", "sW", bdsb[:], bd_in)
        b.wait("vector", "sW", t0)
        b.op("vector", "memset", onec_bf[:], 1.0)
        b.op("vector", "memset", onerb[:], 1.0)
        b.op("vector", "memset", oner_f[:], 1.0, sem="sDVE")

        cv = Carve(pool)
        p0_dma = 0
        zsb = cv.take(5504, U8, rows=86)
        zu8 = cv.take(11008, U8, rows=86)
        ssb = cv.take(44032, U8, rows=86)
        obf = cv.take(22016, U8, rows=86)
        vsb = cv.take(22016, U8, rows=86)

        for (qz, sdev, sbf_dst, v_dst, nrow, ncol) in (
            (qzg, sg_dev, sgbf, vg_d, 32, I),
            (qzu, su_dev, subf, vu_d, 32, I),
            (qzd, sd_dev, sdbf, vd_d, 86, H),
        ):
            npk = ncol // 8
            half = ncol // 2
            b.wait("sync", "sDVE", b.cnt["sDVE"])
            tz = b.dma("sync", "sW", zsb[0:nrow, 0:npk * 4].bitcast(I32), qz)
            ts = b.dma("sync", "sW", ssb[0:nrow, 0:ncol * 4].bitcast(F32), sdev)
            b.wait("vector", "sW", max(tz, ts))
            b.wait("vector", "sXS", p0_dma)
            zz = zsb[0:nrow, 0:npk * 4]
            b.op("vector", "tensor_single_scalar", zu8[0:nrow, 0:half], zz, 15, ALU.bitwise_and)
            b.op("vector", "tensor_single_scalar", zu8[0:nrow, half:ncol], zz, 4, ALU.logical_shift_right)
            sview = ssb[0:nrow, 0:ncol * 4].bitcast(F32)
            b.op("vector", "tensor_copy", obf[0:nrow, 0:ncol * 2].bitcast(BF16), sview)
            b.op("vector", "tensor_tensor", vsb[0:nrow, 0:ncol * 2].bitcast(BF16),
                 zu8[0:nrow, 0:ncol], sview, ALU.mult)
            vv = b.op("vector", "tensor_scalar_mul", vsb[0:nrow, 0:ncol * 2].bitcast(BF16),
                      vsb[0:nrow, 0:ncol * 2].bitcast(BF16), -1.0, sem="sDVE")
            b.wait("gpsimd", "sDVE", vv)
            b.dma("gpsimd", "sXS", sbf_dst, obf[0:nrow, 0:ncol * 2].bitcast(BF16))
            p0_dma = b.dma("gpsimd", "sXS", v_dst, vsb[0:nrow, 0:ncol * 2].bitcast(BF16))
        for (srcB, dstB) in ((sgB_t, sgBbf), (suB_t, suBbf)):
            b.wait("sync", "sDVE", b.cnt["sDVE"])
            tb = b.dma("sync", "sW", ssb[0:86, 0:32768].bitcast(F32), srcB.rearrange("m h n -> m (h n)"))
            b.wait("vector", "sW", tb)
            b.wait("vector", "sXS", p0_dma)
            vv = b.op("vector", "tensor_copy", obf[0:86, 0:16384].bitcast(BF16),
                      ssb[0:86, 0:32768].bitcast(F32), sem="sDVE")
            b.wait("gpsimd", "sDVE", vv)
            p0_dma = b.dma("gpsimd", "sXS", dstB.rearrange("m h n -> m (h n)"),
                           obf[0:86, 0:16384].bitcast(BF16))
        b.barrier()

        def ln_rows(pe_stats, RTMT, rtmt_guard, apply_guard=0):
            b.wait("scalar", "sPE", pe_stats)
            b.op("scalar", "copy", r_sum, ps2[0:1, :])
            a1 = b.op("scalar", "copy", r_sq, ps3[0:1, :], sem="sACT")
            b.wait("vector", "sACT", a1)
            b.op("vector", "tensor_scalar_mul", r_m, r_sum, 1.0 / 4096.0)
            b.op("vector", "tensor_scalar_mul", r_t1, r_sq, 1.0 / 4096.0)
            b.op("vector", "tensor_mul", r_sum, r_m, r_m)
            v1 = b.op("vector", "tensor_sub", r_t1, r_t1, r_sum, sem="sDVE")
            b.wait("scalar", "sDVE", v1)
            b.op("scalar", "activation", r_sq, r_t1, AF.Ln, bias=EPS, scale=1.0)
            a2 = b.op("scalar", "activation", r_rstd, r_sq, AF.Exp, bias=0.0, scale=-0.5, sem="sACT")
            b.wait("vector", "sACT", a2)
            b.op("vector", "tensor_mul", r_mr, r_m, r_rstd)
            b.op("vector", "tensor_copy", rows_bf[0:1, 0:512], r_rstd)
            b.op("vector", "tensor_sub", r_sum, r_rstd, rows_bf[0:1, 0:512])
            b.op("vector", "tensor_copy", rows_bf[0:1, 512:1024], r_sum)
            b.op("vector", "tensor_copy", rows_bf[0:1, 1024:1536], r_mr)
            b.op("vector", "tensor_sub", r_sum, r_mr, rows_bf[0:1, 1024:1536])
            v2 = b.op("vector", "tensor_copy", rows_bf[0:1, 1536:2048], r_sum, sem="sDVE")
            b.wait("tensor", "sDVE", v2)
            b.wait("tensor", "sACT", rtmt_guard)
            b.mm(psA[:, 0:512], onerb[:], rows_bf[0:1, 0:512], start=True, stop=False)
            b.mm(psA[:, 0:512], onerb[:], rows_bf[0:1, 512:1024], start=False, stop=True)
            b.mm(psA[:, 512:1024], onerb[:], rows_bf[0:1, 1024:1536], start=True, stop=False)
            p1 = b.mm(psA[:, 512:1024], onerb[:], rows_bf[0:1, 1536:2048], start=False, stop=True, sem="sPE")
            b.wait("scalar", "sPE", p1)
            b.wait("scalar", "sDVE", apply_guard)
            b.op("scalar", "copy", RTMT[:, 0:512], psA[:, 0:512])
            a3 = b.op("scalar", "copy", RTMT[:, 512:1024], psA[:, 512:1024], sem="sACT")
            return a3, a1

        def ln_proj(xsrc, nsup, oclist, spill_scale, spill_ap, xlnspill):
            cv = Carve(pool)
            XLNP = [cv.take(NF * 512, BF16) for _ in range(2)]
            WST = cv.take(NF * 128, F32)
            WBF = [cv.take(NF * 128, BF16) for _ in range(2)]
            XC = [cv.take(512, F32) for _ in range(4)]
            XBF = [cv.take(512, BF16) for _ in range(2)]
            SQB = [cv.take(512, BF16) for _ in range(2)]
            TTB = [cv.take(512, F32) for _ in range(2)]
            RTMT = cv.take(1024, F32)
            RT = RTMT[:, 0:512]
            MT = RTMT[:, 512:1024]
            EVB = [cv.take(512, BF16) for _ in range(2)]

            pe_stat = [0, 0]
            xc_free = [0, 0, 0, 0]
            evac_done = [0, 0]
            rtmt_guard = 0
            last_apply = 0
            rows_free = 0
            wcast_last = 0
            wcast = [0, 0]
            pe_w = [0, 0]
            evb_dma = [0, 0]
            pe_pair_done = 0

            for pair in range((nsup + 1) // 2):
                sps = list(range(2 * pair, min(2 * pair + 2, nsup)))
                xln_ready = [0, 0]
                for spi, sp in enumerate(sps):
                    ts0 = 512 * sp
                    for fc in range(NF):
                        par = fc % 2
                        xsem = "sQa" if par == 0 else "sQb"
                        b.wait("gpsimd", "sDVE", xc_free[par])
                        tx = b.dma("gpsimd", xsem, XC[par], xsrc[128 * fc:128 * fc + 128, ts0:ts0 + 512])
                        b.wait("vector", xsem, tx)
                        b.wait("vector", "sPE", pe_stat[par])
                        vd = b.op("vector", "tensor_copy", XBF[par], XC[par], sem="sDVE")
                        xc_free[par] = vd
                        b.wait("scalar", "sDVE", vd)
                        ad = b.op("scalar", "activation", SQB[par], XBF[par], AF.Square, sem="sACT")
                        b.wait("tensor", "sACT", ad)
                        if fc == 0:
                            b.wait("tensor", "sACT", rows_free)
                        b.mm(ps2[0:1, :], onec_bf[:], XBF[par], start=(fc == 0), stop=(fc == NF - 1))
                        pe_stat[par] = b.mm(ps3[0:1, :], onec_bf[:], SQB[par],
                                            start=(fc == 0), stop=(fc == NF - 1), sem="sPE")
                    t_rtmt, rows_free = ln_rows(pe_stat[(NF - 1) % 2], RTMT, rtmt_guard,
                                                 apply_guard=last_apply)
                    rtmt_guard = t_rtmt
                    for fc in range(NF):
                        par = fc % 2
                        xsem = "sWa" if par == 0 else "sWb"
                        b.wait("gpsimd", "sDVE", xc_free[2 + par])
                        tx = b.dma("gpsimd", xsem, XC[2 + par], xsrc[128 * fc:128 * fc + 128, ts0:ts0 + 512])
                        b.wait("vector", xsem, tx)
                        b.wait("vector", "sACT", t_rtmt)
                        if fc < 2:
                            b.wait("vector", "sPE", pe_pair_done)
                        b.op("vector", "tensor_mul", TTB[par], XC[2 + par], RT)
                        xc_free[2 + par] = b.cnt["sDVE"] + 1
                        xsl = XLNP[spi][:, fc * 512:(fc + 1) * 512]
                        vv = b.op("vector", "tensor_sub", xsl, TTB[par], MT, sem="sDVE")
                        if xlnspill:
                            b.wait("gpsimd", "sDVE", vv)
                            b.dma("gpsimd", "sXS", xln1sp[128 * fc:128 * fc + 128, ts0:ts0 + 512], xsl)
                    xln_ready[spi] = last_apply = b.cnt["sDVE"]
                psK = [ps0, ps1]
                for oci, oc in enumerate(oclist):
                    par = oci % 2
                    b.wait("sync", "sDVE", wcast_last)
                    tw = b.dma("sync", "sW", WST.rearrange("p (f c) -> p f c", c=128), wqkv_t[oc].rearrange("f p c -> p f c"))
                    b.wait("vector", "sW", tw)
                    b.wait("vector", "sPE", pe_w[par])
                    wcast[par] = wcast_last = b.op("vector", "tensor_copy", WBF[par], WST, sem="sDVE")
                    for spi, sp in enumerate(sps):
                        ts0 = 512 * sp
                        b.wait("tensor", "sDVE", max(wcast[par], xln_ready[spi]))
                        b.wait("tensor", "sACT", evac_done[par])
                        pe = None
                        for f in range(NF):
                            last = (f == NF - 1)
                            pe = b.mm(psK[par][:], WBF[par][:, f * 128:(f + 1) * 128],
                                      XLNP[spi][:, f * 512:(f + 1) * 512],
                                      start=(f == 0), stop=last, sem=("sPE" if last else None))
                        pe_w[par] = pe
                        b.wait("scalar", "sPE", pe)
                        b.wait("scalar", "sSa" if par == 0 else "sSb", evb_dma[par])
                        if spill_scale is None:
                            aev = b.op("scalar", "copy", EVB[par], psK[par][:], sem="sACT")
                        else:
                            aev = b.op("scalar", "activation", EVB[par], psK[par][:], AF.Copy,
                                       bias=0.0, scale=spill_scale, sem="sACT")
                        evac_done[par] = aev
                        b.wait("gpsimd", "sACT", aev)
                        evb_dma[par] = b.dma("gpsimd", "sSa" if par == 0 else "sSb",
                                             spill_ap(oc, ts0), EVB[par])
                pe_pair_done = b.cnt["sPE"]
            b.barrier()

        if upto >= 1:
            ln_proj(x_batchT, 8, list(range(32, 40)), None,
                    lambda oc, ts0: kspill[128 * (oc - 32):128 * (oc - 32) + 128, ts0:ts0 + 512],
                    xlnspill=True)

        if upto >= 2:
            cv = Carve(pool)
            XLN1 = cv.take(NF * 512, BF16)
            WVST = [cv.take(1024, F32) for _ in range(2)]
            WVBF = cv.take(NF * 1024, BF16)
            VEV = [cv.take(512, BF16) for _ in range(2)]
            wcast = 0
            wv_casts = [0, 0]
            for f in range(NF):
                par = f % 2
                b.wait("sync", "sDVE", wv_casts[par])
                tw = b.dma("sync", "sW", WVST[par], wv_t[f])
                b.wait("vector", "sW", tw)
                wcast = wv_casts[par] = b.op(
                    "vector", "tensor_copy",
                    WVBF[:, f * 1024:(f + 1) * 1024], WVST[par], sem="sDVE")
            evac_done = [0, 0]
            vev_dma = [0, 0]
            for sp in range(8):
                ts0 = 512 * sp
                b.wait("gpsimd", "sPE", b.cnt["sPE"])
                xdma = b.dma("gpsimd", "sX", XLN1.rearrange("p (f q) -> p f q", q=512),
                             xln1sp[:, ts0:ts0 + 512].rearrange("(f p) q -> p f q", p=128))
                pi = 0
                for tc in range(4):
                    for vc in range(2):
                        par = pi % 2
                        pi += 1
                        b.wait("tensor", "sX", xdma)
                        b.wait("tensor", "sDVE", wcast)
                        b.wait("tensor", "sACT", evac_done[par])
                        pe = None
                        for f in range(NF):
                            last = (f == NF - 1)
                            pe = b.mm([ps0, ps1][par][:],
                                      XLN1[:, f * 512 + tc * 128: f * 512 + tc * 128 + 128],
                                      WVBF[:, f * 1024 + vc * 512: f * 1024 + vc * 512 + 512],
                                      start=(f == 0), stop=last, sem=("sPE" if last else None))
                        b.wait("scalar", "sPE", pe)
                        b.wait("scalar", "sSa" if par == 0 else "sSb", vev_dma[par])
                        aev = b.op("scalar", "copy", VEV[par], [ps0, ps1][par][:], sem="sACT")
                        evac_done[par] = aev
                        b.wait("gpsimd", "sACT", aev)
                        vev_dma[par] = b.dma(
                            "gpsimd", "sSa" if par == 0 else "sSb",
                            vspill[ts0 + tc * 128: ts0 + tc * 128 + 128, vc * 512:vc * 512 + 512],
                            VEV[par])
            b.barrier()

        if upto >= 3:
            ln_proj(x_myT, 4, list(range(32)), QK_SCALE,
                    lambda oc, ts0: qspill[128 * oc:128 * oc + 128, ts0:ts0 + 512],
                    xlnspill=False)

        if upto >= 4:
            import os as _os
            P4P = int(_os.environ.get("P4P", "4"))
            P4G = int(_os.environ.get("P4G", "8"))
            P4R = int(_os.environ.get("P4R", "2"))
            P4MODE = int(_os.environ.get("P4MODE", "0"))
            cv = Carve(pool)
            SDMB = cv.take(32 * 512, BF16)
            QPR = [cv.take(1024, BF16) for _ in range(2)]
            KALL = [cv.take(32 * 128, BF16) for _ in range(2)]
            VALL = [cv.take(32 * 128, BF16) for _ in range(2)]
            TMP = [cv.take(1024, BF16) for _ in range(2)]
            EALL = cv.take(32 * 1024, BF16)
            OSB = cv.take(NH * 512, BF16)
            OTb = [cv.take(512, F32) for _ in range(2)]
            RTb = [cv.take(512, F32) for _ in range(2)]

            sdm_free = 0
            osb_free = 0
            pvO_free = 0
            ot_free = 0
            st_free = [0, 0]
            kv_free = [0, 0]
            qpr_free = [0, 0]
            eall_free = 0
            psST = [psA, psB]
            for p in range(P4P):
                n = PADDED[p]
                b.wait("sync", "sDVE", sdm_free)
                sdma = b.dma("sync", "sW2", SDMB[:, 0:n * 512].rearrange("p (n q) -> p n q", q=512),
                             sdm_t[CUMPAD[p]:CUMPAD[p] + n].rearrange("n p q -> p n q"))
                for g in range(P4G):
                    gq = (p * 8 + g) % 2
                    b.wait("gpsimd", "sPE", kv_free[gq])
                    kvsem = "sX2" if gq == 0 else "sX3"
                    kall_t = b.dma("gpsimd", kvsem, KALL[gq][:, 0:n * 128],
                                   kspill[128 * g:128 * g + 128, 0:n * 128])
                    vall_t = b.dma("gpsimd", kvsem,
                                   VALL[gq][:, 0:n * 128].rearrange("p (n d) -> p n d", d=128),
                                   vspill[0:n * 128, 128 * g:128 * g + 128].rearrange("(n p) d -> p n d", p=128))
                    for pr in range(P4R):
                        h0 = 4 * g + 2 * pr
                        qsem = "sQa" if pr == 0 else "sQb"
                        b.wait("gpsimd", "sPE", qpr_free[pr])
                        b.dma("gpsimd", qsem, QPR[pr][:, 0:512],
                              qspill[128 * h0:128 * h0 + 128, 512 * p:512 * p + 512])
                        qpr_t = b.dma("gpsimd", qsem, QPR[pr][:, 512:1024],
                                      qspill[128 * h0 + 128:128 * h0 + 256, 512 * p:512 * p + 512])
                        st_done, exp_done, tmp_done = {}, {}, {}
                        for kt in range(n):
                            par = kt % 2
                            b.wait("vector", "sW2", sdma)
                            b.wait("vector", "sPE", st_done.get(kt - 2, 0))
                            b.op("vector", "tensor_scalar_mul", TMP[par][:, 0:512],
                                 SDMB[:, kt * 512:(kt + 1) * 512], -SLOPES[h0])
                            tmp_done[kt] = b.op("vector", "tensor_scalar_mul", TMP[par][:, 512:1024],
                                                SDMB[:, kt * 512:(kt + 1) * 512], -SLOPES[h0 + 1], sem="sDVE")
                            b.wait("tensor", kvsem, kall_t)
                            b.wait("tensor", qsem, qpr_t)
                            b.wait("tensor", "sDVE", tmp_done[kt])
                            b.wait("tensor", "sACT", exp_done.get(kt - 2, st_free[par]))
                            ps = psST[par]
                            b.mm(ps[:, 0:512], KALL[gq][:, kt * 128:(kt + 1) * 128],
                                 QPR[pr][:, 0:512], start=True, stop=False)
                            b.mm(ps[:, 0:512], idbf[:], TMP[par][:, 0:512], start=False, stop=True)
                            b.mm(ps[:, 512:1024], KALL[gq][:, kt * 128:(kt + 1) * 128],
                                 QPR[pr][:, 512:1024], start=True, stop=False)
                            st_done[kt] = b.mm(ps[:, 512:1024], idbf[:], TMP[par][:, 512:1024],
                                               start=False, stop=True, sem="sPE")
                            b.wait("scalar", "sPE", st_done[kt])
                            if kt == 0:
                                b.wait("scalar", "sPE", eall_free)
                            exp_done[kt] = b.op("scalar", "activation",
                                                EALL[:, kt * 1024:(kt + 1) * 1024], psST[par][:],
                                                AF.Exp, bias=-M_EXP, scale=1.0, sem="sACT")
                        if P4MODE == 1:
                            continue
                        b.wait("tensor", "sACT", max(exp_done[n - 1], pvO_free))
                        b.wait("tensor", kvsem, vall_t)
                        qpr_free[pr] = b.cnt["sPE"] + 4
                        for kt in range(n):
                            b.mm(ps0[:], VALL[gq][:, kt * 128:(kt + 1) * 128],
                                 EALL[:, kt * 1024:kt * 1024 + 512], start=(kt == 0), stop=(kt == n - 1),
                                 sem=("sPE" if kt == n - 1 else None))
                        for kt in range(n):
                            b.mm(ps1[:], VALL[gq][:, kt * 128:(kt + 1) * 128],
                                 EALL[:, kt * 1024 + 512:kt * 1024 + 1024], start=(kt == 0), stop=(kt == n - 1),
                                 sem=("sPE" if kt == n - 1 else None))
                        for kt in range(n):
                            b.mm(ps2[0:1, :], onec_bf[:], EALL[:, kt * 1024:kt * 1024 + 512],
                                 start=(kt == 0), stop=(kt == n - 1),
                                 sem=("sPE" if kt == n - 1 else None))
                        pv_all = None
                        for kt in range(n):
                            pv_all = b.mm(ps3[0:1, :], onec_bf[:], EALL[:, kt * 1024 + 512:kt * 1024 + 1024],
                                          start=(kt == 0), stop=(kt == n - 1),
                                          sem=("sPE" if kt == n - 1 else None))
                        eall_free = pv_all
                        if P4MODE == 2:
                            pvO_free = 0
                            continue
                        b.wait("scalar", "sPE", pv_all)
                        b.wait("scalar", "sDVE", ot_free)
                        b.op("scalar", "copy", r_sum, ps2[0:1, :])
                        b.op("scalar", "copy", r_sq, ps3[0:1, :])
                        b.op("scalar", "copy", OTb[0], ps0[:])
                        a_t1 = b.op("scalar", "copy", OTb[1], ps1[:], sem="sACT")
                        pvO_free = a_t1
                        st_free[1] = a_t1
                        if P4MODE == 3:
                            pvO_free = 0
                            st_free[1] = 0
                            continue
                        b.wait("vector", "sACT", a_t1)
                        b.op("vector", "reciprocal", r_m, r_sum)
                        b.op("vector", "reciprocal", r_t1, r_sq)
                        b.op("vector", "tensor_copy", rows_bf[0:1, 0:512], r_m)
                        b.op("vector", "tensor_sub", r_sum, r_m, rows_bf[0:1, 0:512])
                        b.op("vector", "tensor_copy", rows_bf[0:1, 512:1024], r_sum)
                        b.op("vector", "tensor_copy", rows_bf[0:1, 1024:1536], r_t1)
                        b.op("vector", "tensor_sub", r_sq, r_t1, rows_bf[0:1, 1024:1536])
                        v_t1 = b.op("vector", "tensor_copy", rows_bf[0:1, 1536:2048], r_sq, sem="sDVE")
                        b.wait("tensor", "sDVE", v_t1)
                        b.wait("tensor", "sACT", a_t1)
                        b.mm(psA[:, 0:512], onerb[:], rows_bf[0:1, 0:512], start=True, stop=False)
                        b.mm(psA[:, 0:512], onerb[:], rows_bf[0:1, 512:1024], start=False, stop=True)
                        b.mm(psA[:, 512:1024], onerb[:], rows_bf[0:1, 1024:1536], start=True, stop=False)
                        pe_t1 = b.mm(psA[:, 512:1024], onerb[:], rows_bf[0:1, 1536:2048], start=False, stop=True, sem="sPE")
                        if P4MODE == 4:
                            st_free[0] = 0
                            continue
                        b.wait("scalar", "sPE", pe_t1)
                        b.op("scalar", "copy", RTb[0], psA[:, 0:512])
                        a_t2 = b.op("scalar", "copy", RTb[1], psA[:, 512:1024], sem="sACT")
                        st_free[0] = a_t2
                        b.wait("vector", "sACT", a_t2)
                        b.wait("vector", "sXS", osb_free)
                        b.op("vector", "tensor_mul", OSB[:, h0 * 512:(h0 + 1) * 512], OTb[0], RTb[0])
                        ot_free = b.op("vector", "tensor_mul", OSB[:, (h0 + 1) * 512:(h0 + 2) * 512],
                                       OTb[1], RTb[1], sem="sDVE")
                    kv_free[gq] = b.cnt["sPE"]
                sdm_free = b.cnt["sDVE"]
                b.wait("gpsimd", "sDVE", b.cnt["sDVE"])
                osb_free = b.dma("gpsimd", "sXS",
                                 ospill[:, 512 * p:512 * p + 512].rearrange("(hq d) q -> d hq q", d=128),
                                 OSB.rearrange("d (hq q) -> d hq q", q=512))
            b.barrier()

        if upto >= 5:
            cv = Carve(pool)
            QGsb = [cv.take(1024, I32) for _ in range(2)]
            QUsb = [cv.take(1024, I32) for _ in range(2)]
            QDsb = [cv.take(512, I32) for _ in range(2)]
            DU = [cv.take(4096, U8) for _ in range(2)]
            SBT = [cv.take(4096, BF16) for _ in range(2)]
            WPT = [cv.take(4096, BF16) for _ in range(2)]
            SDT = [cv.take(4096, BF16) for _ in range(2)]
            WDP = [cv.take(4096, BF16) for _ in range(2)]

            qg_free = [0, 0]
            sbt_free = [0, 0]
            wpt_free = [0, 0]
            bi = 0
            for j in range(43):
                pj = j % 2
                b.wait("sync", "sDVE", qg_free[pj])
                qg_tgt = b.dma("sync", "sW2", QGsb[pj], qwg_t[j])
                qu_tgt = b.dma("sync", "sW2", QUsb[pj], qwu_t[j])
                for plane in (0, 1):
                    m = plane * 43 + j
                    for QS, sdram, wsp, qtgt in ((QGsb[pj], sgBbf, wgsp, qg_tgt),
                                                 (QUsb[pj], suBbf, wusp, qu_tgt)):
                        par = bi % 2
                        bi += 1
                        ssem = "sWa" if par == 0 else "sWb"
                        b.wait("sync", "sDVE", sbt_free[par])
                        sb_t = 0
                        for hf in (0, 1):
                            sb_t = b.dma("sync", ssem, SBT[par][64 * hf:64 * hf + 64, :],
                                         sdram[m, hf:hf + 1, :].broadcast_to([64, 4096]))
                        b.wait("vector", "sW2", qtgt)
                        b.wait("vector", ssem, sb_t)
                        b.wait("vector", "sSa" if par == 0 else "sSb", wpt_free[par])
                        qv8 = QS.bitcast(U8)
                        if plane == 0:
                            b.op("vector", "tensor_single_scalar", DU[par], qv8, 15, ALU.bitwise_and)
                        else:
                            b.op("vector", "tensor_single_scalar", DU[par], qv8, 4, ALU.logical_shift_right)
                        qg_free[pj] = b.cnt["sDVE"] + 1
                        sbt_free[par] = b.cnt["sDVE"] + 1
                        vmul = b.op("vector", "tensor_tensor", WPT[par], DU[par], SBT[par],
                                    ALU.mult, sem="sDVE")
                        b.wait("gpsimd", "sDVE", vmul)
                        wpt_free[par] = b.dma("gpsimd", "sSa" if par == 0 else "sSb",
                                              wsp[m], WPT[par])
            qd_free = [0, 0]
            sdt_free = [0, 0]
            wdp_free = [0, 0]
            for m in range(NI):
                pm = m % 2
                j = m % 43
                b.wait("sync", "sDVE", qd_free[pm])
                qd_t = b.dma("sync", "sW2", QDsb[pm], qwd_t[m])
                dsem = "sQa" if pm == 0 else "sQb"
                b.wait("sync", "sDVE", sdt_free[pm])
                sd_t = 0
                for hf in (0, 1):
                    dst = SDT[pm][64 * hf:64 * hf + 64, :]
                    src = sdbf[2 * j + hf:2 * j + hf + 1, :].broadcast_to([64, H])
                    sd_t = b.dma("sync", dsem, dst, src)
                b.wait("vector", "sW2", qd_t)
                b.wait("vector", dsem, sd_t)
                b.wait("vector", "sSa" if pm == 0 else "sSb", wdp_free[pm])
                qv8 = QDsb[pm].bitcast(U8)
                b.op("vector", "tensor_single_scalar", DU[pm][:, 0:2048], qv8, 15, ALU.bitwise_and)
                b.op("vector", "tensor_single_scalar", DU[pm][:, 2048:4096], qv8, 4, ALU.logical_shift_right)
                qd_free[pm] = b.cnt["sDVE"] + 1
                sdt_free[pm] = b.cnt["sDVE"] + 1
                vmul = b.op("vector", "tensor_tensor", WDP[pm], DU[pm], SDT[pm], ALU.mult, sem="sDVE")
                b.wait("gpsimd", "sDVE", vmul)
                wdp_free[pm] = b.dma("gpsimd", "sSa" if pm == 0 else "sSb",
                                     wdsp[:, :, 128 * m:128 * m + 128].transpose([1, 0, 2]),
                                     WDP[pm].rearrange("p (hc c) -> p hc c", c=128))
            WOS = [QGsb[0], QGsb[1]]
            woc_free = [0, 0]
            wo_cast = [0, 0]
            for oc in range(32):
                for q4 in range(4):
                    par4 = (oc * 4 + q4) % 2
                    b.wait("sync", "sDVE", wo_cast[par4])
                    twx = b.dma("sync", "sW2", WOS[par4].bitcast(F32),
                                wo_t[oc].rearrange("f p c -> p f c")[:, 8 * q4:8 * q4 + 8, :])
                    b.wait("vector", "sW2", twx)
                    b.wait("vector", "sSa" if par4 == 0 else "sSb", woc_free[par4])
                    wo_cast[par4] = b.op("vector", "tensor_copy",
                                         WPT[par4][:, 0:1024], WOS[par4].bitcast(F32), sem="sDVE")
                    b.wait("gpsimd", "sDVE", wo_cast[par4])
                    woc_free[par4] = b.dma("gpsimd", "sSa" if par4 == 0 else "sSb",
                                           wobf_d[oc, :, 1024 * q4:1024 * q4 + 1024],
                                           WPT[par4][:, 0:1024])
            b.barrier()

        if upto >= 6:
            cv = Carve(pool)
            OSB2 = cv.take(NF * 512, BF16)
            WST2 = cv.take(NF * 128, F32)
            WBF2 = [cv.take(NF * 128, BF16) for _ in range(2)]
            OUT1 = cv.take(NF * 512, F32)
            XRS = [cv.take(512, F32) for _ in range(2)]
            O1T = [cv.take(512, F32) for _ in range(2)]
            O1B = [cv.take(512, BF16) for _ in range(2)]
            SQ2 = [cv.take(512, BF16) for _ in range(2)]
            XL2 = [cv.take(512, BF16) for _ in range(2)]
            RTMT2 = cv.take(1024, F32)
            RT2 = RTMT2[:, 0:512]
            MT2 = RTMT2[:, 512:1024]
            TT2 = [cv.take(512, F32) for _ in range(2)]

            pe_w = [0, 0]
            evac_done = [0, 0]
            stat_pe = [0, 0]
            xr_free = [0, 0]
            rows_free = 0
            rtmt_guard2 = 0
            last_apply2 = 0
            xl2_dma = [0, 0]
            out1_dma_last = 0
            psK = [ps0, ps1]
            for t in range(4):
                ts0 = 512 * t
                b.wait("gpsimd", "sPE", b.cnt["sPE"])
                b.wait("vector", "sXS", out1_dma_last)
                odma = b.dma("gpsimd", "sX", OSB2.rearrange("p (f q) -> p f q", q=512),
                             ospill[:, ts0:ts0 + 512].rearrange("(f p) q -> p f q", p=128))
                pe_stats = 0
                for oc in range(NF):
                    par = oc % 2
                    b.wait("sync", "sPE", pe_w[par])
                    tw = b.dma("sync", "sWa" if par == 0 else "sWb", WBF2[par], wobf_d[oc])
                    b.wait("gpsimd", "sDVE", xr_free[par])
                    xr = b.dma("gpsimd", "sQa" if par == 0 else "sQb",
                               XRS[par], x_myT[128 * oc:128 * oc + 128, ts0:ts0 + 512])
                    b.wait("tensor", "sWa" if par == 0 else "sWb", tw)
                    b.wait("tensor", "sX", odma)
                    b.wait("tensor", "sACT", evac_done[par])
                    pe = None
                    for f in range(NF):
                        last = (f == NF - 1)
                        pe = b.mm(psK[par][:], WBF2[par][:, f * 128:(f + 1) * 128],
                                  OSB2[:, f * 512:(f + 1) * 512],
                                  start=(f == 0), stop=last, sem=("sPE" if last else None))
                    pe_w[par] = pe
                    b.wait("scalar", "sPE", pe)
                    evac_done[par] = b.op("scalar", "copy", O1T[par], psK[par][:], sem="sACT")
                    b.wait("vector", "sACT", evac_done[par])
                    b.wait("vector", "sQa" if par == 0 else "sQb", xr)
                    b.wait("vector", "sPE", stat_pe[par])
                    b.op("vector", "tensor_add", OUT1[:, oc * 512:(oc + 1) * 512], O1T[par], XRS[par])
                    xr_free[par] = b.cnt["sDVE"] + 1
                    v1 = b.op("vector", "tensor_copy", O1B[par], OUT1[:, oc * 512:(oc + 1) * 512], sem="sDVE")
                    b.wait("gpsimd", "sDVE", v1)
                    out1_dma_last = b.dma("gpsimd", "sXS", out1sp[128 * oc:128 * oc + 128, ts0:ts0 + 512],
                                          OUT1[:, oc * 512:(oc + 1) * 512])
                    b.wait("scalar", "sDVE", v1)
                    aq = b.op("scalar", "activation", SQ2[par], O1B[par], AF.Square, sem="sACT")
                    b.wait("tensor", "sACT", aq)
                    if oc == 0:
                        b.wait("tensor", "sACT", rows_free)
                    b.mm(ps2[0:1, :], onec_bf[:], O1B[par], start=(oc == 0), stop=(oc == NF - 1))
                    pe_stats = stat_pe[par] = b.mm(ps3[0:1, :], onec_bf[:], SQ2[par],
                                                   start=(oc == 0), stop=(oc == NF - 1), sem="sPE")
                t_rt2, rows_free = ln_rows(pe_stats, RTMT2, rtmt_guard2,
                                           apply_guard=last_apply2)
                rtmt_guard2 = t_rt2
                for oc in range(NF):
                    par = oc % 2
                    b.wait("vector", "sACT", t_rt2)
                    b.wait("vector", "sSa" if par == 0 else "sSb", xl2_dma[par])
                    b.op("vector", "tensor_mul", TT2[par], OUT1[:, oc * 512:(oc + 1) * 512], RT2)
                    v2 = b.op("vector", "tensor_sub", XL2[par], TT2[par], MT2, sem="sDVE")
                    last_apply2 = v2
                    b.wait("gpsimd", "sDVE", v2)
                    xl2_dma[par] = b.dma("gpsimd", "sSa" if par == 0 else "sSb",
                                         xln2sp[128 * oc:128 * oc + 128, ts0:ts0 + 512], XL2[par])
            b.barrier()

        if upto >= 7:
            cv = Carve(pool)
            XLN2 = cv.take(NF * 2048, BF16)
            WG = [cv.take(4096, BF16) for _ in range(2)]
            WU = [cv.take(4096, BF16) for _ in range(2)]
            HB = cv.take(2048, BF16)
            SGB = [cv.take(512, BF16) for _ in range(2)]
            UPB = [cv.take(512, BF16) for _ in range(2)]

            for fc in range(NF):
                b.dma("gpsimd", "sX", XLN2[:, fc * 2048:(fc + 1) * 2048],
                      xln2sp[128 * fc:128 * fc + 128, :])
            xl_tgt = b.cnt["sX"]
            xs_guard = 0
            for t in range(4):
                b.wait("tensor", "sX", xl_tgt)
                b.wait("tensor", "sACT", xs_guard)
                pe = None
                for fc in range(NF):
                    last = (fc == NF - 1)
                    pe = b.mm(ps2[0:32, :], b2sb[:, (fc % 16) * 32:(fc % 16) * 32 + 32],
                              XLN2[:, fc * 2048 + 512 * t: fc * 2048 + 512 * t + 512],
                              start=(fc == 0), stop=last, sem=("sPE" if last else None))
                b.wait("scalar", "sPE", pe)
                xs_guard = b.op("scalar", "copy", xs2sb[:, 512 * t:512 * t + 512], ps2[0:32, :], sem="sACT")
            a_xs2 = xs_guard
            if dbg:
                b.wait("gpsimd", "sACT", a_xs2)
                b.dma("gpsimd", "sX", xs2dump, xs2sb[:])

            w_free = [0, 0]
            gu_free = [[0, 0], [0, 0]]
            hb_dma = 0
            psG = [ps0, ps1]
            for m in range(NI):
                pm = m % 2
                wsem = "sWa" if pm == 0 else "sWb"
                b.wait("sync", "sPE", w_free[pm])
                wgt = b.dma("sync", wsem, WG[pm], wgsp[m])
                wut = b.dma("sync", wsem, WU[pm], wusp[m])
                vg_t = b.dma("sync", wsem, vgt[:, pm * 128:pm * 128 + 128], vg_d[:, 128 * m:128 * m + 128])
                vu_t = b.dma("sync", wsem, vut[:, pm * 128:pm * 128 + 128], vu_d[:, 128 * m:128 * m + 128])
                pe_last = None
                for t in range(4):
                    pt = t % 2
                    b.wait("tensor", wsem, max(wgt, vg_t))
                    b.wait("tensor", "sACT", max(a_xs2, gu_free[0][pt]))
                    for f in range(NF):
                        b.mm(psG[pt][:], WG[pm][:, f * 128:(f + 1) * 128],
                             XLN2[:, f * 2048 + 512 * t: f * 2048 + 512 * t + 512],
                             start=(f == 0), stop=False)
                    peg = b.mm(psG[pt][:], vgt[0:32, pm * 128:pm * 128 + 128], xs2sb[:, 512 * t:512 * t + 512],
                               start=False, stop=True, sem="sPE")
                    b.wait("tensor", wsem, max(wut, vu_t))
                    b.wait("tensor", "sACT", gu_free[1][pt])
                    for f in range(NF):
                        b.mm(psA[:, pt * 512:pt * 512 + 512], WU[pm][:, f * 128:(f + 1) * 128],
                             XLN2[:, f * 2048 + 512 * t: f * 2048 + 512 * t + 512],
                             start=(f == 0), stop=False)
                    peu = b.mm(psA[:, pt * 512:pt * 512 + 512], vut[0:32, pm * 128:pm * 128 + 128],
                               xs2sb[:, 512 * t:512 * t + 512], start=False, stop=True, sem="sPE")
                    pe_last = peu
                    b.wait("scalar", "sPE", peu)
                    b.op("scalar", "activation", SGB[pt], psG[pt][:], AF.Silu, bias=0.0, scale=1.0)
                    gu_free[0][pt] = gu_free[1][pt] = aev = b.op(
                        "scalar", "copy", UPB[pt], psA[:, pt * 512:pt * 512 + 512], sem="sACT")
                    if dbg and m == 0:
                        b.wait("gpsimd", "sACT", aev)
                        b.dma("gpsimd", "sX", hdump[:, 1024 * t:1024 * t + 512], SGB[pt])
                        b.dma("gpsimd", "sX", hdump[:, 1024 * t + 512:1024 * t + 1024], UPB[pt])
                        b.wait("vector", "sX", b.cnt["sX"])
                    b.wait("vector", "sACT", aev)
                    b.wait("vector", "sXS", hb_dma)
                    b.op("vector", "tensor_mul", HB[:, 512 * t:512 * t + 512], SGB[pt], UPB[pt],
                         sem="sDVE")
                w_free[pm] = pe_last
                b.wait("gpsimd", "sDVE", b.cnt["sDVE"])
                hb_dma = b.dma("gpsimd", "sXS", hspill[128 * m:128 * m + 128, :], HB)
            b.barrier()

        if upto >= 8:
            cv = Carve(pool)
            HSL = cv.take(NI * 512, BF16)
            WDC = [cv.take(NI * 128, BF16) for _ in range(2)]
            OU1 = [cv.take(512, F32) for _ in range(2)]
            DTb = [cv.take(512, F32) for _ in range(2)]
            FBb = [cv.take(512, F32) for _ in range(2)]

            wd_free = [0, 0]
            dt_free = [0, 0]
            fb_dma = [0, 0]
            ou_free = [0, 0]
            hs_guard = 0
            psD = [ps0, ps1]
            for t in range(4):
                ts0 = 512 * t
                b.wait("gpsimd", "sPE", b.cnt["sPE"])
                hd = b.dma("gpsimd", "sX", HSL.rearrange("p (m q) -> p m q", q=512),
                           hspill[:, ts0:ts0 + 512].rearrange("(m p) q -> p m q", p=128))
                b.wait("tensor", "sX", hd)
                b.wait("tensor", "sACT", hs_guard)
                pe = None
                for m in range(NI):
                    last = (m == NI - 1)
                    pe = b.mm(psA[0:86, 0:512], bdsb[:, (m % 43) * 86:(m % 43) * 86 + 86],
                              HSL[:, m * 512:(m + 1) * 512],
                              start=(m == 0), stop=last, sem=("sPE" if last else None))
                b.wait("scalar", "sPE", pe)
                hs_guard = b.op("scalar", "copy", hssb[:], psA[0:86, 0:512], sem="sACT")
                a_hs = hs_guard
                for hc in range(32):
                    ph = hc % 2
                    wsem8 = "sWa" if ph == 0 else "sWb"
                    b.wait("sync", "sPE", wd_free[ph])
                    wd_t = b.dma("sync", wsem8, WDC[ph], wdsp[hc])
                    vd_t = b.dma("sync", wsem8, vdt[:, ph * 128:ph * 128 + 128],
                                 vd_d[:, 128 * hc:128 * hc + 128])
                    b.wait("gpsimd", "sDVE", ou_free[ph])
                    ou_t = b.dma("gpsimd", "sQa" if ph == 0 else "sQb",
                                 OU1[ph], out1sp[128 * hc:128 * hc + 128, ts0:ts0 + 512])
                    b.wait("tensor", wsem8, max(wd_t, vd_t))
                    b.wait("tensor", "sACT", max(a_hs, dt_free[ph]))
                    for m in range(NI):
                        b.mm(psD[ph][:], WDC[ph][:, m * 128:(m + 1) * 128],
                             HSL[:, m * 512:(m + 1) * 512], start=(m == 0), stop=False)
                    pe_d = b.mm(psD[ph][:], vdt[0:86, ph * 128:ph * 128 + 128], hssb[:],
                                start=False, stop=True, sem="sPE")
                    wd_free[ph] = pe_d
                    b.wait("scalar", "sPE", pe_d)
                    dt_free[ph] = b.op("scalar", "copy", DTb[ph], psD[ph][:], sem="sACT")
                    b.wait("vector", "sACT", dt_free[ph])
                    b.wait("vector", "sQa" if ph == 0 else "sQb", ou_t)
                    b.wait("vector", "sSa" if ph == 0 else "sSb", fb_dma[ph])
                    vf = b.op("vector", "tensor_add", FBb[ph], DTb[ph], OU1[ph], sem="sDVE")
                    ou_free[ph] = vf
                    b.wait("gpsimd", "sDVE", vf)
                    fb_dma[ph] = b.dma("gpsimd", "sSa" if ph == 0 else "sSb",
                                       outT[128 * hc:128 * hc + 128, ts0:ts0 + 512], FBb[ph])

        b.wait("gpsimd", "sX", b.cnt["sX"])
        b.wait("gpsimd", "sX2", b.cnt["sX2"])
        b.wait("gpsimd", "sX3", b.cnt["sX3"])
        b.wait("gpsimd", "sXS", b.cnt["sXS"])
        b.wait("gpsimd", "sX4", b.cnt["sX4"])
        b.wait("gpsimd", "sWa", b.cnt["sWa"])
        b.wait("gpsimd", "sWb", b.cnt["sWb"])
        b.wait("gpsimd", "sQa", b.cnt["sQa"])
        b.wait("gpsimd", "sQb", b.cnt["sQb"])
        b.wait("gpsimd", "sSa", b.cnt["sSa"])
        b.wait("gpsimd", "sSb", b.cnt["sSb"])

        @block.sync
        def _(e):
            for f in b.streams["sync"]:
                f(e)

        @block.vector
        def _(e):
            for f in b.streams["vector"]:
                f(e)

        @block.scalar
        def _(e):
            for f in b.streams["scalar"]:
                f(e)

        @block.tensor
        def _(e):
            for f in b.streams["tensor"]:
                f(e)

        @block.gpsimd
        def _(e):
            for f in b.streams["gpsimd"]:
                f(e)

    return nc


def _pack_shared(wqkv, wo, gqw, gsc, gqz, uqw, usc, uqz, dqw, dsc, dqz):
    wqkv_p = np.ascontiguousarray(wqkv[PERM_H, :])
    wqkv_t = np.ascontiguousarray(
        wqkv_p[:, :5120].reshape(32, 128, 40, 128).transpose(2, 0, 1, 3))
    wv_t = np.ascontiguousarray(wqkv_p[:, 5120:].reshape(32, 128, 1024))
    wo_t = np.ascontiguousarray(wo[:, PERM_H].reshape(32, 128, 32, 128).transpose(2, 0, 1, 3))
    qwg_t = np.ascontiguousarray(
        gqw[PERM_H, :].reshape(32, 128, 43, 32).transpose(2, 1, 0, 3).reshape(43, 128, 1024))
    qwu_t = np.ascontiguousarray(
        uqw[PERM_H, :].reshape(32, 128, 43, 32).transpose(2, 1, 0, 3).reshape(43, 128, 1024))
    qwd_t = np.ascontiguousarray(dqw[PERM_I, :].reshape(86, 128, 512))
    sg_dev = np.ascontiguousarray(gsc[:, PERM_I])
    su_dev = np.ascontiguousarray(usc[:, PERM_I])
    sd_dev = np.ascontiguousarray(dsc[:, PERM_H])

    cidx = np.arange(4096)
    grow = 2 * ((cidx // 128) % 16)
    tcol = cidx % 128
    sgB = np.empty((86, 2, 4096), dtype=np.float32)
    suB = np.empty((86, 2, 4096), dtype=np.float32)
    for m in range(86):
        for hf in range(2):
            sgB[m, hf] = sg_dev[grow + hf, 128 * m + tcol]
            suB[m, hf] = su_dev[grow + hf, 128 * m + tcol]

    p = np.arange(128)
    b2 = np.zeros((128, 512), dtype=np.float32)
    for j in range(16):
        b2[p, j * 32 + 2 * j + (p >= 64)] = 1.0
    bd = np.zeros((128, 43 * 86), dtype=np.float32)
    for j in range(43):
        bd[p, j * 86 + 2 * j + (p >= 64)] = 1.0
    cstv = np.zeros((1, 16), dtype=np.float32)
    cstv[0, :len(CONSTS)] = CONSTS

    return {
        "wqkv_t": wqkv_t, "wv_t": wv_t, "wo_t": wo_t,
        "qwg_t": qwg_t, "qwu_t": qwu_t, "qwd_t": qwd_t,
        "sg_dev": sg_dev, "su_dev": su_dev, "sd_dev": sd_dev,
        "sgB_t": sgB, "suB_t": suB,
        "qzg": np.ascontiguousarray(gqz), "qzu": np.ascontiguousarray(uqz),
        "qzd": np.ascontiguousarray(dqz),
        "b2_in": b2.astype(BF), "bd_in": bd.astype(BF),
        "id_in": np.eye(128, dtype=np.float32).astype(BF),
        "cst_in": cstv,
    }


def _sdm_table(half):
    out = np.empty((NKT, 128, 512), dtype=np.float32)
    for pos, blk in enumerate(MYBLOCKS[half]):
        q0 = 512 * blk
        for kt in range(PADDED[pos]):
            k = 128 * kt + np.arange(128)[:, None]
            q = q0 + np.arange(512)[None, :]
            rel = q - k
            v = np.where(rel >= 0, np.sqrt(np.maximum(rel, 0).astype(np.float64)), 1e30)
            out[CUMPAD[pos] + kt] = v.astype(np.float32)
    return out.astype(BF)


def make_inmaps(x, wqkv, wo, gqw, gsc, gqz, uqw, usc, uqz, dqw, dsc, dqz, ncore=8):
    shared = _pack_shared(wqkv, wo, gqw, gsc, gqz, uqw, usc, uqz, dqw, dsc, dqz)
    sdm = [_sdm_table(h) for h in range(2)]
    in_maps = []
    my_toks = []
    for c in range(ncore):
        bb, hh = c // 2, c % 2
        xT = x[bb].T[PERM_H, :]
        my_tok = np.concatenate([np.arange(512 * blk, 512 * blk + 512) for blk in MYBLOCKS[hh]])
        my_toks.append(my_tok)
        m = dict(shared)
        m["x_batchT"] = np.ascontiguousarray(xT.astype(np.float32))
        m["x_myT"] = np.ascontiguousarray(xT[:, my_tok].astype(np.float32))
        m["sdm_t"] = sdm[hh]
        in_maps.append(m)
    return in_maps, my_toks


def kernel(L0_attn_hidden_states, L0_attn_w_qkv, L0_attn_w_o,
           L0_ffn_gate_qweight, L0_ffn_gate_scales, L0_ffn_gate_qzeros,
           L0_ffn_up_qweight, L0_ffn_up_scales, L0_ffn_up_qzeros,
           L0_ffn_down_qweight, L0_ffn_down_scales, L0_ffn_down_qzeros):
    in_maps, my_toks = make_inmaps(
        np.asarray(L0_attn_hidden_states),
        np.asarray(L0_attn_w_qkv), np.asarray(L0_attn_w_o),
        np.asarray(L0_ffn_gate_qweight), np.asarray(L0_ffn_gate_scales),
        np.asarray(L0_ffn_gate_qzeros),
        np.asarray(L0_ffn_up_qweight), np.asarray(L0_ffn_up_scales),
        np.asarray(L0_ffn_up_qzeros),
        np.asarray(L0_ffn_down_qweight), np.asarray(L0_ffn_down_scales),
        np.asarray(L0_ffn_down_qzeros))

    nc = build_program(upto=8)
    res = run_bass_kernel_spmd(nc, in_maps, core_ids=list(range(8)))

    out = np.empty((B, S, H), dtype=np.float32)
    for c in range(8):
        bb = c // 2
        o = np.asarray(res.results[c]["outT"])
        tmp = np.empty((H, TOK), dtype=np.float32)
        tmp[PERM_H, :] = o
        out[bb, my_toks[c], :] = tmp.T
    return out


# version 07
# speedup vs baseline: 1.1040x; 1.1020x over previous
import numpy as np
import ml_dtypes
import concourse.bass as bass
import concourse.mybir as mybir
from concourse.bass_utils import run_bass_kernel_spmd

F32 = mybir.dt.float32
BF16 = mybir.dt.bfloat16
I32 = mybir.dt.int32
U8 = mybir.dt.uint8
AF = mybir.ActivationFunctionType
ALU = mybir.AluOpType
BF = ml_dtypes.bfloat16

B, S, H, NH, NKV, D, I = 4, 4096, 4096, 32, 8, 128, 11008
TOK = 2048
NF = H // 128
NI = I // 128
M_EXP = 30.0
QK_SCALE = D ** -0.5
SLOPES = [2.0 ** (-0.25 * (hq + 1)) for hq in range(NH)]
MYBLOCKS = [[0, 3, 4, 7], [1, 2, 5, 6]]
PADDED = [8, 16, 24, 32]
CUMPAD = [0, 8, 24, 48]
NKT = 80
PAT = [[0, 4, 1, 5], [2, 6, 3, 7]]
EPS = 1e-5
POOL_LIM = 180224

ENGINES = ("sync", "vector", "scalar", "tensor", "gpsimd")


def _perm(n_j):
    n = n_j * 256
    p = np.empty(n, dtype=np.int64)
    for plane in range(2):
        for j in range(n_j):
            t = np.arange(128)
            p[plane * n_j * 128 + 128 * j + t] = 8 * (32 * j + t // 4) + np.array(PAT[plane])[t % 4]
    return p


PERM_H = _perm(16)
PERM_I = _perm(43)
CONSTS = [-M_EXP, EPS, -0.5, 1.0 / 4096.0, QK_SCALE, 1.0, 0.0, -1.0]


class Bld:
    def __init__(self, nc, sems):
        self.nc = nc
        self.sems = sems
        self.streams = {e: [] for e in ENGINES}
        self.cnt = {k: 0 for k in sems}

    def raw(self, eng, fn):
        self.streams[eng].append(fn)

    def wait(self, eng, sem, val):
        if val is None or val <= 0:
            return
        s = self.sems[sem]
        self.raw(eng, lambda e, s=s, v=val: e.wait_ge(s, v))

    def dma(self, eng, sem, out, in_):
        s = self.sems[sem]
        self.cnt[sem] += 16
        tgt = self.cnt[sem]
        self.raw(eng, lambda e, o=out, i=in_, s=s: e.dma_start(out=o, in_=i).then_inc(s, 16))
        return tgt

    def op(self, eng, meth, *a, sem=None, **kw):
        s = tgt = None
        if sem is not None:
            s = self.sems[sem]
            self.cnt[sem] += 1
            tgt = self.cnt[sem]

        def f(e, meth=meth, a=a, kw=kw, s=s):
            inst = getattr(e, meth)(*a, **kw)
            if s is not None:
                inst.then_inc(s, 1)
        self.raw(eng, f)
        return tgt

    def mm(self, out, lhsT, rhs, start, stop, sem=None):
        return self.op("tensor", "matmul", out, lhsT, rhs, start=start, stop=stop, sem=sem)

    def barrier(self):
        snap = dict(self.cnt)
        for e in ENGINES:
            for k, v in snap.items():
                self.wait(e, k, v)


def ceil4(x):
    return (x + 3) & ~3


class Carve:
    def __init__(self, pool):
        self.pool = pool
        self.off = 0

    def take(self, nelem, dtype, rows=128):
        size = {F32: 4, BF16: 2, I32: 4, U8: 1}[dtype]
        nbytes = ceil4(nelem * size)
        assert self.off + nbytes <= POOL_LIM, f"pool overflow {self.off + nbytes}"
        v = self.pool[0:rows, self.off:self.off + nbytes]
        if dtype != U8:
            v = v.bitcast(dtype)
        self.off += nbytes
        return v


def build_program(upto=8, dbg=False):
    nc = bass.Bass("TRN2", target_bir_lowering=False)
    dt = nc.dram_tensor
    DK = "ExternalOutput" if dbg else "Internal"

    x_batchT = dt("x_batchT", [H, S], F32, kind="ExternalInput").ap()
    x_myT = dt("x_myT", [H, TOK], F32, kind="ExternalInput").ap()
    wqkv_t = dt("wqkv_t", [40, 128, 32, 128], F32, kind="ExternalInput").ap()
    wv_t = dt("wv_t", [32, 128, 1024], F32, kind="ExternalInput").ap()
    wo_t = dt("wo_t", [32, 128, 32, 128], F32, kind="ExternalInput").ap()
    qwg_t = dt("qwg_t", [43, 128, 1024], I32, kind="ExternalInput").ap()
    qwu_t = dt("qwu_t", [43, 128, 1024], I32, kind="ExternalInput").ap()
    qwd_t = dt("qwd_t", [86, 128, 512], I32, kind="ExternalInput").ap()
    sg_dev = dt("sg_dev", [32, I], F32, kind="ExternalInput").ap()
    su_dev = dt("su_dev", [32, I], F32, kind="ExternalInput").ap()
    sgB_t = dt("sgB_t", [86, 2, 4096], F32, kind="ExternalInput").ap()
    suB_t = dt("suB_t", [86, 2, 4096], F32, kind="ExternalInput").ap()
    sd_dev = dt("sd_dev", [86, H], F32, kind="ExternalInput").ap()
    qzg = dt("qzg", [32, 1376], I32, kind="ExternalInput").ap()
    qzu = dt("qzu", [32, 1376], I32, kind="ExternalInput").ap()
    qzd = dt("qzd", [86, 512], I32, kind="ExternalInput").ap()
    sdm_t = dt("sdm_t", [NKT, 128, 512], BF16, kind="ExternalInput").ap()
    b2_in = dt("b2_in", [128, 512], BF16, kind="ExternalInput").ap()
    bd_in = dt("bd_in", [128, 43 * 86], BF16, kind="ExternalInput").ap()
    id_in = dt("id_in", [128, 128], BF16, kind="ExternalInput").ap()
    cst_in = dt("cst_in", [1, 16], F32, kind="ExternalInput").ap()

    outT = dt("outT", [H, TOK], F32, kind="ExternalOutput").ap()

    kspill = dt("kspill", [NKV * D, S], BF16, kind=DK).ap()
    vspill = dt("vspill", [S, NKV * D], BF16, kind=DK).ap()
    qspill = dt("qspill", [NH * D, TOK], BF16, kind=DK).ap()
    ospill = dt("ospill", [NH * D, TOK], BF16, kind=DK).ap()
    xln1sp = dt("xln1sp", [H, S], BF16, kind=DK).ap()
    out1sp = dt("out1sp", [H, TOK], F32, kind=DK).ap()
    xln2sp = dt("xln2sp", [H, TOK], BF16, kind=DK).ap()
    hspill = dt("hspill", [I, TOK], BF16, kind=DK).ap()
    wgsp = dt("wgsp", [NI, 128, 4096], BF16, kind=DK).ap()
    wusp = dt("wusp", [NI, 128, 4096], BF16, kind=DK).ap()
    wdsp = dt("wdsp", [32, 128, NI * 128], BF16, kind=DK).ap()
    sgbf = dt("sgbf", [32, I], BF16, kind="Internal").ap()
    subf = dt("subf", [32, I], BF16, kind="Internal").ap()
    sgBbf = dt("sgBbf", [86, 2, 4096], BF16, kind="Internal").ap()
    wobf_d = dt("wobf_d", [32, 128, 4096], BF16, kind="Internal").ap()
    suBbf = dt("suBbf", [86, 2, 4096], BF16, kind="Internal").ap()
    sdbf = dt("sdbf", [86, H], BF16, kind="Internal").ap()
    rowsp = dt("rowsp", [1, 1024], F32, kind="Internal").ap()
    xs2dump = dt("xs2dump", [32, TOK], BF16, kind=DK).ap()
    hdump = dt("hdump", [128, 4096], BF16, kind=DK).ap()
    vg_d = dt("vg_d", [32, I], BF16, kind=DK).ap()
    vu_d = dt("vu_d", [32, I], BF16, kind=DK).ap()
    vd_d = dt("vd_d", [86, H], BF16, kind=DK).ap()

    from contextlib import ExitStack
    with ExitStack() as stack:
        ec = stack.enter_context
        cst = ec(nc.sbuf_tensor("cst", [128, 16], F32))
        idbf = ec(nc.sbuf_tensor("idbf", [128, 128], BF16))
        onec_bf = ec(nc.sbuf_tensor("onec_bf", [128, 1], BF16))
        oner_f = ec(nc.sbuf_tensor("oner_f", [1, 128], F32))
        onerb = ec(nc.sbuf_tensor("onerb", [1, 128], BF16))
        rows_bf = ec(nc.sbuf_tensor("rows_bf", [1, 2048], BF16))
        rows = ec(nc.sbuf_tensor("rows", [1, 3072], F32))
        b2sb = ec(nc.sbuf_tensor("b2sb", [128, 512], BF16))
        bdsb = ec(nc.sbuf_tensor("bdsb", [128, 43 * 86], BF16))
        xs2sb = ec(nc.sbuf_tensor("xs2sb", [32, TOK], BF16))
        hssb = ec(nc.sbuf_tensor("hssb", [86, 512], BF16))
        vgt = ec(nc.sbuf_tensor("vgt", [32, 256], BF16))
        vut = ec(nc.sbuf_tensor("vut", [32, 256], BF16))
        vdt = ec(nc.sbuf_tensor("vdt", [86, 256], BF16))
        pool = ec(nc.sbuf_tensor("pool", [128, POOL_LIM], U8))
        psA = ec(nc.psum_tensor("psA", [128, 1024], F32))
        psB = ec(nc.psum_tensor("psB", [128, 1024], F32))
        ps0 = ec(nc.psum_tensor("ps0", [128, 512], F32))
        ps1 = ec(nc.psum_tensor("ps1", [128, 512], F32))
        ps2 = ec(nc.psum_tensor("ps2", [128, 512], F32))
        ps3 = ec(nc.psum_tensor("ps3", [128, 512], F32))
        _sW = ec(nc.semaphore("sW"))
        _sW2 = ec(nc.semaphore("sW2"))
        _sX = ec(nc.semaphore("sX"))
        _sX2 = ec(nc.semaphore("sX2"))
        _sX3 = ec(nc.semaphore("sX3"))
        _sX4 = ec(nc.semaphore("sX4"))
        _sXS = ec(nc.semaphore("sXS"))
        _sWa = ec(nc.semaphore("sWa"))
        _sWb = ec(nc.semaphore("sWb"))
        _sQa = ec(nc.semaphore("sQa"))
        _sQb = ec(nc.semaphore("sQb"))
        _sSa = ec(nc.semaphore("sSa"))
        _sSb = ec(nc.semaphore("sSb"))
        _sDq1 = ec(nc.semaphore("sDq1"))
        _sDq2 = ec(nc.semaphore("sDq2"))
        _sQc = ec(nc.semaphore("sQc"))
        _sQd = ec(nc.semaphore("sQd"))
        _sWc = ec(nc.semaphore("sWc"))
        _sWd = ec(nc.semaphore("sWd"))
        _sWE = ec(nc.semaphore("sWE"))
        _sPE = ec(nc.semaphore("sPE"))
        _sACT = ec(nc.semaphore("sACT"))
        _sDVE = ec(nc.semaphore("sDVE"))
        block = ec(nc.Block())
        for ci, v in enumerate(CONSTS):
            nc.const_aps.register(F32, v, cst[:, ci:ci + 1])

        sems = {"sW": _sW, "sW2": _sW2, "sX": _sX, "sX2": _sX2, "sX3": _sX3,
                "sX4": _sX4, "sXS": _sXS, "sWa": _sWa, "sWb": _sWb,
                "sQa": _sQa, "sQb": _sQb, "sSa": _sSa, "sSb": _sSb,
                "sDq1": _sDq1, "sDq2": _sDq2, "sQc": _sQc, "sQd": _sQd,
                "sWc": _sWc, "sWd": _sWd, "sWE": _sWE,
                "sPE": _sPE, "sACT": _sACT, "sDVE": _sDVE}
        b = Bld(nc, sems)

        def row(i):
            return rows[0:1, i * 512:(i + 1) * 512]
        r_sum, r_sq, r_m, r_t1, r_rstd, r_mr = (row(i) for i in range(6))

        b.dma("sync", "sW", cst[:], cst_in.broadcast_to([128, 16]))
        b.dma("sync", "sW", idbf[:], id_in)
        b.dma("sync", "sW", b2sb[:], b2_in)
        t0 = b.dma("sync", "sW", bdsb[:], bd_in)
        b.wait("vector", "sW", t0)
        b.op("vector", "memset", onec_bf[:], 1.0)
        b.op("vector", "memset", onerb[:], 1.0)
        b.op("vector", "memset", oner_f[:], 1.0, sem="sDVE")

        cv = Carve(pool)
        p0_dma = 0
        zsb = cv.take(5504, U8, rows=86)
        zu8 = cv.take(11008, U8, rows=86)
        ssb = cv.take(44032, U8, rows=86)
        obf = cv.take(22016, U8, rows=86)
        vsb = cv.take(22016, U8, rows=86)

        for (qz, sdev, sbf_dst, v_dst, nrow, ncol) in (
            (qzg, sg_dev, sgbf, vg_d, 32, I),
            (qzu, su_dev, subf, vu_d, 32, I),
            (qzd, sd_dev, sdbf, vd_d, 86, H),
        ):
            npk = ncol // 8
            half = ncol // 2
            b.wait("sync", "sDVE", b.cnt["sDVE"])
            tz = b.dma("sync", "sW", zsb[0:nrow, 0:npk * 4].bitcast(I32), qz)
            ts = b.dma("sync", "sW", ssb[0:nrow, 0:ncol * 4].bitcast(F32), sdev)
            b.wait("vector", "sW", max(tz, ts))
            b.wait("vector", "sXS", p0_dma)
            zz = zsb[0:nrow, 0:npk * 4]
            b.op("vector", "tensor_single_scalar", zu8[0:nrow, 0:half], zz, 15, ALU.bitwise_and)
            b.op("vector", "tensor_single_scalar", zu8[0:nrow, half:ncol], zz, 4, ALU.logical_shift_right)
            sview = ssb[0:nrow, 0:ncol * 4].bitcast(F32)
            b.op("vector", "tensor_copy", obf[0:nrow, 0:ncol * 2].bitcast(BF16), sview)
            b.op("vector", "tensor_tensor", vsb[0:nrow, 0:ncol * 2].bitcast(BF16),
                 zu8[0:nrow, 0:ncol], sview, ALU.mult)
            vv = b.op("vector", "tensor_scalar_mul", vsb[0:nrow, 0:ncol * 2].bitcast(BF16),
                      vsb[0:nrow, 0:ncol * 2].bitcast(BF16), -1.0, sem="sDVE")
            b.wait("gpsimd", "sDVE", vv)
            b.dma("gpsimd", "sXS", sbf_dst, obf[0:nrow, 0:ncol * 2].bitcast(BF16))
            p0_dma = b.dma("gpsimd", "sXS", v_dst, vsb[0:nrow, 0:ncol * 2].bitcast(BF16))
        for (srcB, dstB) in ((sgB_t, sgBbf), (suB_t, suBbf)):
            b.wait("sync", "sDVE", b.cnt["sDVE"])
            tb = 0
            for q4 in range(4):
                tb = b.dma("sync", "sW", ssb[0:86, 8192 * q4:8192 * q4 + 8192].bitcast(F32),
                           srcB.rearrange("m h n -> m (h n)")[:, 2048 * q4:2048 * q4 + 2048])
            b.wait("vector", "sW", tb)
            b.wait("vector", "sXS", p0_dma)
            vv = b.op("vector", "tensor_copy", obf[0:86, 0:16384].bitcast(BF16),
                      ssb[0:86, 0:32768].bitcast(F32), sem="sDVE")
            b.wait("gpsimd", "sDVE", vv)
            for h2 in range(2):
                p0_dma = b.dma("gpsimd", "sXS",
                               dstB.rearrange("m h n -> m (h n)")[:, 4096 * h2:4096 * h2 + 4096],
                               obf[0:86, 8192 * h2:8192 * h2 + 8192].bitcast(BF16))
        b.barrier()

        def ln_rows(pe_stats, RTMT, rtmt_guard, apply_guard=0):
            b.wait("scalar", "sPE", pe_stats)
            b.op("scalar", "copy", r_sum, ps2[0:1, :])
            a1 = b.op("scalar", "copy", r_sq, ps3[0:1, :], sem="sACT")
            b.wait("vector", "sACT", a1)
            b.op("vector", "tensor_scalar_mul", r_m, r_sum, 1.0 / 4096.0)
            b.op("vector", "tensor_scalar_mul", r_t1, r_sq, 1.0 / 4096.0)
            b.op("vector", "tensor_mul", r_sum, r_m, r_m)
            v1 = b.op("vector", "tensor_sub", r_t1, r_t1, r_sum, sem="sDVE")
            b.wait("scalar", "sDVE", v1)
            b.op("scalar", "activation", r_sq, r_t1, AF.Ln, bias=EPS, scale=1.0)
            a2 = b.op("scalar", "activation", r_rstd, r_sq, AF.Exp, bias=0.0, scale=-0.5, sem="sACT")
            b.wait("vector", "sACT", a2)
            b.op("vector", "tensor_mul", r_mr, r_m, r_rstd)
            b.op("vector", "tensor_copy", rows_bf[0:1, 0:512], r_rstd)
            b.op("vector", "tensor_sub", r_sum, r_rstd, rows_bf[0:1, 0:512])
            b.op("vector", "tensor_copy", rows_bf[0:1, 512:1024], r_sum)
            b.op("vector", "tensor_copy", rows_bf[0:1, 1024:1536], r_mr)
            b.op("vector", "tensor_sub", r_sum, r_mr, rows_bf[0:1, 1024:1536])
            v2 = b.op("vector", "tensor_copy", rows_bf[0:1, 1536:2048], r_sum, sem="sDVE")
            b.wait("tensor", "sDVE", v2)
            b.wait("tensor", "sACT", rtmt_guard)
            b.mm(psA[:, 0:512], onerb[:], rows_bf[0:1, 0:512], start=True, stop=False)
            b.mm(psA[:, 0:512], onerb[:], rows_bf[0:1, 512:1024], start=False, stop=True)
            b.mm(psA[:, 512:1024], onerb[:], rows_bf[0:1, 1024:1536], start=True, stop=False)
            p1 = b.mm(psA[:, 512:1024], onerb[:], rows_bf[0:1, 1536:2048], start=False, stop=True, sem="sPE")
            b.wait("scalar", "sPE", p1)
            b.wait("scalar", "sDVE", apply_guard)
            b.op("scalar", "copy", RTMT[:, 0:512], psA[:, 0:512])
            a3 = b.op("scalar", "copy", RTMT[:, 512:1024], psA[:, 512:1024], sem="sACT")
            return a3, a1

        SQ4 = ["sQa", "sQb", "sQc", "sQd"]
        SW4 = ["sWa", "sWb", "sWc", "sWd"]

        def ln_proj(xsrc, nsup, oclist, spill_scale, spill_ap, xlnspill, pump=None):
            cv = Carve(pool)
            XLNP = [cv.take(NF * 512, BF16) for _ in range(2)]
            WSTh = [cv.take(NF * 64, F32) for _ in range(2)]
            WBF = [cv.take(NF * 128, BF16) for _ in range(2)]
            XCs = [cv.take(512, F32) for _ in range(4)]
            XBF = [cv.take(512, BF16) for _ in range(2)]
            SQB = [cv.take(512, BF16) for _ in range(2)]
            TTB = [cv.take(512, F32) for _ in range(2)]
            RTMT = cv.take(1024, F32)
            RT = RTMT[:, 0:512]
            MT = RTMT[:, 512:1024]
            EVB = [cv.take(512, BF16) for _ in range(2)]
            XCa = [cv.take(512, F32) for _ in range(4)]

            pe_stat = [0, 0]
            xc_free = [0, 0, 0, 0]
            xca_free = [0, 0, 0, 0]
            evac_done = [0, 0]
            rtmt_guard = 0
            last_apply = 0
            rows_free = 0
            wcast = [0, 0]
            pe_w = [0, 0]
            evb_dma = [0, 0]
            pe_pair_done = 0
            WSEM = ["sW", "sWE"]

            for pair in range((nsup + 1) // 2):
                sps = list(range(2 * pair, min(2 * pair + 2, nsup)))
                xln_ready = [0, 0]
                for spi, sp in enumerate(sps):
                    ts0 = 512 * sp
                    for fc in range(NF):
                        par = fc % 2
                        sl = fc % 4
                        xsem = SQ4[sl]
                        b.wait("gpsimd", "sDVE", xc_free[sl])
                        b.dma("gpsimd", xsem, XCs[sl][:, 0:256],
                              xsrc[128 * fc:128 * fc + 128, ts0:ts0 + 256])
                        tx = b.dma("gpsimd", xsem, XCs[sl][:, 256:512],
                                   xsrc[128 * fc:128 * fc + 128, ts0 + 256:ts0 + 512])
                        b.wait("vector", xsem, tx)
                        b.wait("vector", "sPE", pe_stat[par])
                        vd = b.op("vector", "tensor_copy", XBF[par], XCs[sl], sem="sDVE")
                        xc_free[sl] = vd
                        b.wait("scalar", "sDVE", vd)
                        ad = b.op("scalar", "activation", SQB[par], XBF[par], AF.Square, sem="sACT")
                        b.wait("tensor", "sACT", ad)
                        if fc == 0:
                            b.wait("tensor", "sACT", rows_free)
                        b.mm(ps2[0:1, :], onec_bf[:], XBF[par], start=(fc == 0), stop=(fc == NF - 1))
                        pe_stat[par] = b.mm(ps3[0:1, :], onec_bf[:], SQB[par],
                                            start=(fc == 0), stop=(fc == NF - 1), sem="sPE")
                    t_rtmt, rows_free = ln_rows(pe_stat[(NF - 1) % 2], RTMT, rtmt_guard,
                                                 apply_guard=last_apply)
                    rtmt_guard = t_rtmt
                    for fc in range(NF):
                        par = fc % 2
                        sl = fc % 4
                        xsem = SW4[sl]
                        b.wait("gpsimd", "sDVE", xca_free[sl])
                        b.dma("gpsimd", xsem, XCa[sl][:, 0:256],
                              xsrc[128 * fc:128 * fc + 128, ts0:ts0 + 256])
                        tx = b.dma("gpsimd", xsem, XCa[sl][:, 256:512],
                                   xsrc[128 * fc:128 * fc + 128, ts0 + 256:ts0 + 512])
                        b.wait("vector", xsem, tx)
                        b.wait("vector", "sACT", t_rtmt)
                        if fc < 2:
                            b.wait("vector", "sPE", pe_pair_done)
                        b.op("vector", "tensor_mul", TTB[par], XCa[sl], RT)
                        xca_free[sl] = b.cnt["sDVE"] + 1
                        xsl = XLNP[spi][:, fc * 512:(fc + 1) * 512]
                        vv = b.op("vector", "tensor_sub", xsl, TTB[par], MT, sem="sDVE")
                        if xlnspill:
                            b.wait("gpsimd", "sDVE", vv)
                            b.dma("gpsimd", "sXS", xln1sp[128 * fc:128 * fc + 128, ts0:ts0 + 512], xsl)
                    xln_ready[spi] = last_apply = b.cnt["sDVE"]
                psK = [ps0, ps1]
                for oci, oc in enumerate(oclist):
                    par = oci % 2
                    for hf in (0, 1):
                        wsem = WSEM[hf]
                        b.wait("sync", "sDVE", wcast[hf])
                        tw = 0
                        for q in range(4):
                            tw = b.dma("sync", wsem, WSTh[hf][:, 512 * q:512 * q + 512],
                                       wqkv_t[oc][:, 16 * hf + 4 * q:16 * hf + 4 * q + 4, :]
                                       .rearrange("p f c -> p (f c)"))
                        b.wait("vector", wsem, tw)
                        if hf == 0:
                            b.wait("vector", "sPE", pe_w[par])
                        wcast[hf] = b.op("vector", "tensor_copy",
                                         WBF[par][:, 2048 * hf:2048 * hf + 2048], WSTh[hf],
                                         sem="sDVE")
                    wc = wcast[1]
                    for spi, sp in enumerate(sps):
                        ts0 = 512 * sp
                        b.wait("tensor", "sDVE", max(wc, xln_ready[spi]))
                        b.wait("tensor", "sACT", evac_done[par])
                        pe = None
                        for f in range(NF):
                            last = (f == NF - 1)
                            pe = b.mm(psK[par][:], WBF[par][:, f * 128:(f + 1) * 128],
                                      XLNP[spi][:, f * 512:(f + 1) * 512],
                                      start=(f == 0), stop=last, sem=("sPE" if last else None))
                        pe_w[par] = pe
                        b.wait("scalar", "sPE", pe)
                        b.wait("scalar", "sSa" if par == 0 else "sSb", evb_dma[par])
                        if spill_scale is None:
                            aev = b.op("scalar", "copy", EVB[par], psK[par][:], sem="sACT")
                        else:
                            aev = b.op("scalar", "activation", EVB[par], psK[par][:], AF.Copy,
                                       bias=0.0, scale=spill_scale, sem="sACT")
                        evac_done[par] = aev
                        b.wait("gpsimd", "sACT", aev)
                        evb_dma[par] = b.dma("gpsimd", "sSa" if par == 0 else "sSb",
                                             spill_ap(oc, ts0), EVB[par])
                        if pump is not None:
                            pump()
                pe_pair_done = b.cnt["sPE"]
            while pump is not None and pump():
                pass
            b.barrier()

        def make_down_pump():
            cvd = Carve(pool)
            cvd.off = 129024
            QDp = [cvd.take(512, I32) for _ in range(2)]
            DUp = [cvd.take(4096, U8) for _ in range(2)]
            SDTp = [cvd.take(4096, BF16) for _ in range(2)]
            WDPp = [cvd.take(4096, BF16) for _ in range(2)]
            state = {"m": 0, "qfree": [0, 0], "sfree": [0, 0], "wfree": [0, 0]}

            def pump():
                m = state["m"]
                if m >= NI:
                    return False
                state["m"] = m + 1
                pm = m % 2
                j = m % 43
                qdsem = "sX2" if pm == 0 else "sX3"
                sdsem = "sW2" if pm == 0 else "sX4"
                stsem = "sDq1" if pm == 0 else "sDq2"
                b.wait("sync", "sDVE", state["qfree"][pm])
                qd_t = b.dma("sync", qdsem, QDp[pm], qwd_t[m])
                b.wait("sync", "sDVE", state["sfree"][pm])
                sd_t = 0
                for hf in (0, 1):
                    sd_t = b.dma("sync", sdsem, SDTp[pm][64 * hf:64 * hf + 64, :],
                                 sdbf[2 * j + hf:2 * j + hf + 1, :].broadcast_to([64, H]))
                b.wait("vector", qdsem, qd_t)
                b.wait("vector", sdsem, sd_t)
                b.wait("vector", stsem, state["wfree"][pm])
                qv8 = QDp[pm].bitcast(U8)
                b.op("vector", "tensor_single_scalar", DUp[pm][:, 0:2048], qv8, 15, ALU.bitwise_and)
                b.op("vector", "tensor_single_scalar", DUp[pm][:, 2048:4096], qv8, 4, ALU.logical_shift_right)
                state["qfree"][pm] = b.cnt["sDVE"] + 1
                state["sfree"][pm] = b.cnt["sDVE"] + 1
                vmul = b.op("vector", "tensor_tensor", WDPp[pm], DUp[pm], SDTp[pm], ALU.mult, sem="sDVE")
                b.wait("scalar", "sDVE", vmul)
                t = 0
                for hc in range(32):
                    t = b.dma("scalar", stsem, wdsp[hc][:, 128 * m:128 * m + 128],
                              WDPp[pm][:, 128 * hc:128 * hc + 128])
                state["wfree"][pm] = t
                return True
            return pump

        def make_gu_pump():
            cvg = Carve(pool)
            cvg.off = 129024
            QBp = [cvg.take(1024, I32) for _ in range(2)]
            DUg = [cvg.take(4096, U8) for _ in range(2)]
            SBTg = [cvg.take(4096, BF16) for _ in range(2)]
            WPTg = [cvg.take(4096, BF16) for _ in range(2)]
            state = {"k": 0, "qfree": [0, 0], "sfree": [0, 0], "wfree": [0, 0]}

            def pump():
                k = state["k"]
                if k >= 2 * NI:
                    return False
                state["k"] = k + 1
                m, mat = k // 2, k % 2
                pk = k % 2
                j, plane = m % 43, m // 43
                qsrc = qwg_t[j] if mat == 0 else qwu_t[j]
                sdram = sgBbf if mat == 0 else suBbf
                wsp = wgsp if mat == 0 else wusp
                qlsem = "sX2" if pk == 0 else "sX3"
                sbsem = "sW2" if pk == 0 else "sX4"
                stsem = "sDq1" if pk == 0 else "sDq2"
                QB = QBp
                b.wait("sync", "sDVE", state["qfree"][pk])
                q_t = b.dma("sync", qlsem, QB[pk], qsrc)
                b.wait("sync", "sDVE", state["sfree"][pk])
                sb_t = 0
                for hf in (0, 1):
                    sb_t = b.dma("sync", sbsem, SBTg[pk][64 * hf:64 * hf + 64, :],
                                 sdram[m, hf:hf + 1, :].broadcast_to([64, 4096]))
                b.wait("vector", qlsem, q_t)
                b.wait("vector", sbsem, sb_t)
                b.wait("vector", stsem, state["wfree"][pk])
                qv8 = QB[pk].bitcast(U8)
                if plane == 0:
                    b.op("vector", "tensor_single_scalar", DUg[pk], qv8, 15, ALU.bitwise_and)
                else:
                    b.op("vector", "tensor_single_scalar", DUg[pk], qv8, 4, ALU.logical_shift_right)
                state["qfree"][pk] = b.cnt["sDVE"] + 1
                state["sfree"][pk] = b.cnt["sDVE"] + 1
                vmul = b.op("vector", "tensor_tensor", WPTg[pk], DUg[pk], SBTg[pk], ALU.mult, sem="sDVE")
                b.wait("gpsimd", "sDVE", vmul)
                state["wfree"][pk] = b.dma("gpsimd", stsem, wsp[m], WPTg[pk])
                return True
            return pump

        if upto >= 1:
            ln_proj(x_batchT, 8, list(range(32, 40)), None,
                    lambda oc, ts0: kspill[128 * (oc - 32):128 * (oc - 32) + 128, ts0:ts0 + 512],
                    xlnspill=True, pump=make_down_pump() if upto >= 8 else None)

        if upto >= 2:
            cv = Carve(pool)
            XLN1 = cv.take(NF * 512, BF16)
            WVST = [cv.take(1024, F32) for _ in range(2)]
            WVBF = cv.take(NF * 1024, BF16)
            VEV = [cv.take(512, BF16) for _ in range(2)]
            WOS = [cv.take(1024, I32) for _ in range(2)]
            WOB = [cv.take(1024, BF16) for _ in range(2)]
            wo_state = {"k": 0, "qfree": [0, 0], "wfree": [0, 0]}

            def wo_pump():
                k = wo_state["k"]
                if k >= 128:
                    return False
                wo_state["k"] = k + 1
                oc, q4 = k // 4, k % 4
                par4 = k % 2
                wosem = "sX2" if par4 == 0 else "sX3"
                stsem = "sDq1" if par4 == 0 else "sDq2"
                b.wait("sync", "sDVE", wo_state["qfree"][par4])
                twx = b.dma("sync", wosem, WOS[par4].bitcast(F32).rearrange("p (f c) -> p f c", c=128),
                            wo_t[oc][:, 8 * q4:8 * q4 + 8, :])
                b.wait("vector", wosem, twx)
                b.wait("vector", stsem, wo_state["wfree"][par4])
                cst_t = b.op("vector", "tensor_copy", WOB[par4], WOS[par4].bitcast(F32), sem="sDVE")
                wo_state["qfree"][par4] = cst_t
                b.wait("gpsimd", "sDVE", cst_t)
                wo_state["wfree"][par4] = b.dma("gpsimd", stsem,
                                                wobf_d[oc, :, 1024 * q4:1024 * q4 + 1024], WOB[par4])
                return True
            return_unused = None
            wcast = 0
            wv_casts = [0, 0]
            WVSEM = ["sW", "sWE"]
            for f in range(NF):
                par = f % 2
                b.wait("sync", "sDVE", wv_casts[par])
                tw = b.dma("sync", WVSEM[par], WVST[par][:, 0:512], wv_t[f][:, 0:512])
                tw = b.dma("sync", WVSEM[par], WVST[par][:, 512:1024], wv_t[f][:, 512:1024])
                b.wait("vector", WVSEM[par], tw)
                wcast = wv_casts[par] = b.op(
                    "vector", "tensor_copy",
                    WVBF[:, f * 1024:(f + 1) * 1024], WVST[par], sem="sDVE")
            evac_done = [0, 0]
            vev_dma = [0, 0]
            for sp in range(8):
                ts0 = 512 * sp
                b.wait("gpsimd", "sPE", b.cnt["sPE"])
                xdma = 0
                for fg in range(4):
                    xdma = b.dma("gpsimd", "sX",
                                 XLN1.rearrange("p (f q) -> p f q", q=512)[:, 8 * fg:8 * fg + 8, :],
                                 xln1sp[1024 * fg:1024 * fg + 1024, ts0:ts0 + 512]
                                 .rearrange("(f p) q -> p f q", p=128))
                pi = 0
                for tc in range(4):
                    for vc in range(2):
                        par = pi % 2
                        pi += 1
                        b.wait("tensor", "sX", xdma)
                        b.wait("tensor", "sDVE", wcast)
                        b.wait("tensor", "sACT", evac_done[par])
                        pe = None
                        for f in range(NF):
                            last = (f == NF - 1)
                            pe = b.mm([ps0, ps1][par][:],
                                      XLN1[:, f * 512 + tc * 128: f * 512 + tc * 128 + 128],
                                      WVBF[:, f * 1024 + vc * 512: f * 1024 + vc * 512 + 512],
                                      start=(f == 0), stop=last, sem=("sPE" if last else None))
                        b.wait("scalar", "sPE", pe)
                        b.wait("scalar", "sSa" if par == 0 else "sSb", vev_dma[par])
                        aev = b.op("scalar", "copy", VEV[par], [ps0, ps1][par][:], sem="sACT")
                        evac_done[par] = aev
                        b.wait("gpsimd", "sACT", aev)
                        vev_dma[par] = b.dma(
                            "gpsimd", "sSa" if par == 0 else "sSb",
                            vspill[ts0 + tc * 128: ts0 + tc * 128 + 128, vc * 512:vc * 512 + 512],
                            VEV[par])
                        wo_pump()
                        wo_pump()
            while wo_pump():
                pass
            b.barrier()

        if upto >= 3:
            ln_proj(x_myT, 4, list(range(32)), QK_SCALE,
                    lambda oc, ts0: qspill[128 * oc:128 * oc + 128, ts0:ts0 + 512],
                    xlnspill=False, pump=make_gu_pump() if upto >= 8 else None)

        if upto >= 4:
            import os as _os
            P4P = int(_os.environ.get("P4P", "4"))
            P4G = int(_os.environ.get("P4G", "8"))
            P4R = int(_os.environ.get("P4R", "2"))
            P4MODE = int(_os.environ.get("P4MODE", "0"))
            cv = Carve(pool)
            SDMB = cv.take(32 * 512, BF16)
            QPR = [cv.take(1024, BF16) for _ in range(2)]
            KALL = [cv.take(32 * 128, BF16) for _ in range(2)]
            VALL = [cv.take(32 * 128, BF16) for _ in range(2)]
            TMP = [cv.take(1024, BF16) for _ in range(2)]
            EALL = cv.take(32 * 1024, BF16)
            OSB = cv.take(NH * 512, BF16)
            OTb = [cv.take(512, F32) for _ in range(2)]
            RTb = [cv.take(512, F32) for _ in range(2)]

            sdm_free = 0
            osb_free = 0
            pvO_free = 0
            ot_free = 0
            st_free = [0, 0]
            kv_free = [0, 0]
            qpr_free = [0, 0]
            eall_free = 0
            psST = [psA, psB]
            for p in range(P4P):
                n = PADDED[p]
                nq = n // 4
                b.wait("sync", "sDVE", sdm_free)
                sdma = 0
                for q4 in range(4):
                    a, b2_ = q4 * nq, (q4 + 1) * nq
                    sdma = b.dma("sync", "sW2",
                                 SDMB[:, a * 512:b2_ * 512].rearrange("p (n q) -> p n q", q=512),
                                 sdm_t[CUMPAD[p] + a:CUMPAD[p] + b2_].rearrange("n p q -> p n q"))
                for g in range(P4G):
                    gq = (p * 8 + g) % 2
                    b.wait("gpsimd", "sPE", kv_free[gq])
                    kvsem = "sX2" if gq == 0 else "sX3"
                    kall_t = 0
                    for h2 in range(2):
                        kall_t = b.dma("gpsimd", kvsem,
                                       KALL[gq][:, h2 * n * 64:(h2 + 1) * n * 64],
                                       kspill[128 * g:128 * g + 128, h2 * n * 64:(h2 + 1) * n * 64])
                    vall_t = 0
                    for q4 in range(4):
                        a, b2_ = q4 * nq, (q4 + 1) * nq
                        vall_t = b.dma("gpsimd", kvsem,
                                       VALL[gq][:, a * 128:b2_ * 128].rearrange("p (n d) -> p n d", d=128),
                                       vspill[a * 128:b2_ * 128, 128 * g:128 * g + 128].rearrange("(n p) d -> p n d", p=128))
                    for pr in range(P4R):
                        h0 = 4 * g + 2 * pr
                        qsem = "sQa" if pr == 0 else "sQb"
                        b.wait("gpsimd", "sPE", qpr_free[pr])
                        b.dma("gpsimd", qsem, QPR[pr][:, 0:512],
                              qspill[128 * h0:128 * h0 + 128, 512 * p:512 * p + 512])
                        qpr_t = b.dma("gpsimd", qsem, QPR[pr][:, 512:1024],
                                      qspill[128 * h0 + 128:128 * h0 + 256, 512 * p:512 * p + 512])
                        st_done, exp_done, tmp_done = {}, {}, {}
                        for kt in range(n):
                            par = kt % 2
                            b.wait("vector", "sW2", sdma)
                            b.wait("vector", "sPE", st_done.get(kt - 2, 0))
                            b.op("vector", "tensor_scalar_mul", TMP[par][:, 0:512],
                                 SDMB[:, kt * 512:(kt + 1) * 512], -SLOPES[h0])
                            tmp_done[kt] = b.op("vector", "tensor_scalar_mul", TMP[par][:, 512:1024],
                                                SDMB[:, kt * 512:(kt + 1) * 512], -SLOPES[h0 + 1], sem="sDVE")
                            b.wait("tensor", kvsem, kall_t)
                            b.wait("tensor", qsem, qpr_t)
                            b.wait("tensor", "sDVE", tmp_done[kt])
                            b.wait("tensor", "sACT", exp_done.get(kt - 2, st_free[par]))
                            ps = psST[par]
                            b.mm(ps[:, 0:512], KALL[gq][:, kt * 128:(kt + 1) * 128],
                                 QPR[pr][:, 0:512], start=True, stop=False)
                            b.mm(ps[:, 0:512], idbf[:], TMP[par][:, 0:512], start=False, stop=True)
                            b.mm(ps[:, 512:1024], KALL[gq][:, kt * 128:(kt + 1) * 128],
                                 QPR[pr][:, 512:1024], start=True, stop=False)
                            st_done[kt] = b.mm(ps[:, 512:1024], idbf[:], TMP[par][:, 512:1024],
                                               start=False, stop=True, sem="sPE")
                            b.wait("scalar", "sPE", st_done[kt])
                            if kt == 0:
                                b.wait("scalar", "sPE", eall_free)
                            exp_done[kt] = b.op("scalar", "activation",
                                                EALL[:, kt * 1024:(kt + 1) * 1024], psST[par][:],
                                                AF.Exp, bias=-M_EXP, scale=1.0, sem="sACT")
                        if P4MODE == 1:
                            continue
                        b.wait("tensor", "sACT", max(exp_done[n - 1], pvO_free))
                        b.wait("tensor", kvsem, vall_t)
                        qpr_free[pr] = b.cnt["sPE"] + 4
                        for kt in range(n):
                            b.mm(ps0[:], VALL[gq][:, kt * 128:(kt + 1) * 128],
                                 EALL[:, kt * 1024:kt * 1024 + 512], start=(kt == 0), stop=(kt == n - 1),
                                 sem=("sPE" if kt == n - 1 else None))
                        for kt in range(n):
                            b.mm(ps1[:], VALL[gq][:, kt * 128:(kt + 1) * 128],
                                 EALL[:, kt * 1024 + 512:kt * 1024 + 1024], start=(kt == 0), stop=(kt == n - 1),
                                 sem=("sPE" if kt == n - 1 else None))
                        for kt in range(n):
                            b.mm(ps2[0:1, :], onec_bf[:], EALL[:, kt * 1024:kt * 1024 + 512],
                                 start=(kt == 0), stop=(kt == n - 1),
                                 sem=("sPE" if kt == n - 1 else None))
                        pv_all = None
                        for kt in range(n):
                            pv_all = b.mm(ps3[0:1, :], onec_bf[:], EALL[:, kt * 1024 + 512:kt * 1024 + 1024],
                                          start=(kt == 0), stop=(kt == n - 1),
                                          sem=("sPE" if kt == n - 1 else None))
                        eall_free = pv_all
                        if P4MODE == 2:
                            pvO_free = 0
                            continue
                        b.wait("scalar", "sPE", pv_all)
                        b.wait("scalar", "sDVE", ot_free)
                        b.op("scalar", "copy", r_sum, ps2[0:1, :])
                        b.op("scalar", "copy", r_sq, ps3[0:1, :])
                        b.op("scalar", "copy", OTb[0], ps0[:])
                        a_t1 = b.op("scalar", "copy", OTb[1], ps1[:], sem="sACT")
                        pvO_free = a_t1
                        st_free[1] = a_t1
                        if P4MODE == 3:
                            pvO_free = 0
                            st_free[1] = 0
                            continue
                        b.wait("vector", "sACT", a_t1)
                        b.op("vector", "reciprocal", r_m, r_sum)
                        b.op("vector", "reciprocal", r_t1, r_sq)
                        b.op("vector", "tensor_copy", rows_bf[0:1, 0:512], r_m)
                        b.op("vector", "tensor_sub", r_sum, r_m, rows_bf[0:1, 0:512])
                        b.op("vector", "tensor_copy", rows_bf[0:1, 512:1024], r_sum)
                        b.op("vector", "tensor_copy", rows_bf[0:1, 1024:1536], r_t1)
                        b.op("vector", "tensor_sub", r_sq, r_t1, rows_bf[0:1, 1024:1536])
                        v_t1 = b.op("vector", "tensor_copy", rows_bf[0:1, 1536:2048], r_sq, sem="sDVE")
                        b.wait("tensor", "sDVE", v_t1)
                        b.wait("tensor", "sACT", a_t1)
                        b.mm(psA[:, 0:512], onerb[:], rows_bf[0:1, 0:512], start=True, stop=False)
                        b.mm(psA[:, 0:512], onerb[:], rows_bf[0:1, 512:1024], start=False, stop=True)
                        b.mm(psA[:, 512:1024], onerb[:], rows_bf[0:1, 1024:1536], start=True, stop=False)
                        pe_t1 = b.mm(psA[:, 512:1024], onerb[:], rows_bf[0:1, 1536:2048], start=False, stop=True, sem="sPE")
                        if P4MODE == 4:
                            st_free[0] = 0
                            continue
                        b.wait("scalar", "sPE", pe_t1)
                        b.op("scalar", "copy", RTb[0], psA[:, 0:512])
                        a_t2 = b.op("scalar", "copy", RTb[1], psA[:, 512:1024], sem="sACT")
                        st_free[0] = a_t2
                        b.wait("vector", "sACT", a_t2)
                        b.wait("vector", "sXS", osb_free)
                        b.op("vector", "tensor_mul", OSB[:, h0 * 512:(h0 + 1) * 512], OTb[0], RTb[0])
                        ot_free = b.op("vector", "tensor_mul", OSB[:, (h0 + 1) * 512:(h0 + 2) * 512],
                                       OTb[1], RTb[1], sem="sDVE")
                    kv_free[gq] = b.cnt["sPE"]
                sdm_free = b.cnt["sDVE"]
                b.wait("gpsimd", "sDVE", b.cnt["sDVE"])
                osb_free = b.dma("gpsimd", "sXS",
                                 ospill[:, 512 * p:512 * p + 512].rearrange("(hq d) q -> d hq q", d=128),
                                 OSB.rearrange("d (hq q) -> d hq q", q=512))
            b.barrier()

        if upto >= 6:
            cv = Carve(pool)
            OSB2 = cv.take(NF * 512, BF16)
            WST2 = cv.take(NF * 128, F32)
            WBF2 = [cv.take(NF * 128, BF16) for _ in range(2)]
            OUT1 = cv.take(NF * 512, F32)
            XRS = [cv.take(512, F32) for _ in range(2)]
            O1T = [cv.take(512, F32) for _ in range(2)]
            O1B = [cv.take(512, BF16) for _ in range(2)]
            SQ2 = [cv.take(512, BF16) for _ in range(2)]
            XL2 = [cv.take(512, BF16) for _ in range(2)]
            RTMT2 = cv.take(1024, F32)
            RT2 = RTMT2[:, 0:512]
            MT2 = RTMT2[:, 512:1024]
            TT2 = [cv.take(512, F32) for _ in range(2)]

            pe_w = [0, 0]
            evac_done = [0, 0]
            stat_pe = [0, 0]
            xr_free = [0, 0]
            rows_free = 0
            rtmt_guard2 = 0
            last_apply2 = 0
            xl2_dma = [0, 0]
            out1_dma_last = 0
            psK = [ps0, ps1]
            for t in range(4):
                ts0 = 512 * t
                b.wait("gpsimd", "sPE", b.cnt["sPE"])
                b.wait("vector", "sXS", out1_dma_last)
                odma = 0
                for fg in range(4):
                    odma = b.dma("gpsimd", "sX",
                                 OSB2.rearrange("p (f q) -> p f q", q=512)[:, 8 * fg:8 * fg + 8, :],
                                 ospill[1024 * fg:1024 * fg + 1024, ts0:ts0 + 512]
                                 .rearrange("(f p) q -> p f q", p=128))
                pe_stats = 0
                for oc in range(NF):
                    par = oc % 2
                    b.wait("sync", "sPE", pe_w[par])
                    tw = 0
                    for q4 in range(4):
                        tw = b.dma("sync", "sWa" if par == 0 else "sWb",
                                   WBF2[par][:, 1024 * q4:1024 * q4 + 1024],
                                   wobf_d[oc, :, 1024 * q4:1024 * q4 + 1024])
                    b.wait("gpsimd", "sDVE", xr_free[par])
                    xr = b.dma("gpsimd", "sQa" if par == 0 else "sQb",
                               XRS[par][:, 0:256], x_myT[128 * oc:128 * oc + 128, ts0:ts0 + 256])
                    xr = b.dma("gpsimd", "sQa" if par == 0 else "sQb",
                               XRS[par][:, 256:512], x_myT[128 * oc:128 * oc + 128, ts0 + 256:ts0 + 512])
                    b.wait("tensor", "sWa" if par == 0 else "sWb", tw)
                    b.wait("tensor", "sX", odma)
                    b.wait("tensor", "sACT", evac_done[par])
                    pe = None
                    for f in range(NF):
                        last = (f == NF - 1)
                        pe = b.mm(psK[par][:], WBF2[par][:, f * 128:(f + 1) * 128],
                                  OSB2[:, f * 512:(f + 1) * 512],
                                  start=(f == 0), stop=last, sem=("sPE" if last else None))
                    pe_w[par] = pe
                    b.wait("scalar", "sPE", pe)
                    evac_done[par] = b.op("scalar", "copy", O1T[par], psK[par][:], sem="sACT")
                    b.wait("vector", "sACT", evac_done[par])
                    b.wait("vector", "sQa" if par == 0 else "sQb", xr)
                    b.wait("vector", "sPE", stat_pe[par])
                    b.op("vector", "tensor_add", OUT1[:, oc * 512:(oc + 1) * 512], O1T[par], XRS[par])
                    xr_free[par] = b.cnt["sDVE"] + 1
                    v1 = b.op("vector", "tensor_copy", O1B[par], OUT1[:, oc * 512:(oc + 1) * 512], sem="sDVE")
                    b.wait("gpsimd", "sDVE", v1)
                    out1_dma_last = b.dma("gpsimd", "sXS", out1sp[128 * oc:128 * oc + 128, ts0:ts0 + 512],
                                          OUT1[:, oc * 512:(oc + 1) * 512])
                    b.wait("scalar", "sDVE", v1)
                    aq = b.op("scalar", "activation", SQ2[par], O1B[par], AF.Square, sem="sACT")
                    b.wait("tensor", "sACT", aq)
                    if oc == 0:
                        b.wait("tensor", "sACT", rows_free)
                    b.mm(ps2[0:1, :], onec_bf[:], O1B[par], start=(oc == 0), stop=(oc == NF - 1))
                    pe_stats = stat_pe[par] = b.mm(ps3[0:1, :], onec_bf[:], SQ2[par],
                                                   start=(oc == 0), stop=(oc == NF - 1), sem="sPE")
                t_rt2, rows_free = ln_rows(pe_stats, RTMT2, rtmt_guard2,
                                           apply_guard=last_apply2)
                rtmt_guard2 = t_rt2
                for oc in range(NF):
                    par = oc % 2
                    b.wait("vector", "sACT", t_rt2)
                    b.wait("vector", "sSa" if par == 0 else "sSb", xl2_dma[par])
                    b.op("vector", "tensor_mul", TT2[par], OUT1[:, oc * 512:(oc + 1) * 512], RT2)
                    v2 = b.op("vector", "tensor_sub", XL2[par], TT2[par], MT2, sem="sDVE")
                    last_apply2 = v2
                    b.wait("gpsimd", "sDVE", v2)
                    xl2_dma[par] = b.dma("gpsimd", "sSa" if par == 0 else "sSb",
                                         xln2sp[128 * oc:128 * oc + 128, ts0:ts0 + 512], XL2[par])
            b.barrier()

        if upto >= 7:
            cv = Carve(pool)
            XLN2 = cv.take(NF * 2048, BF16)
            WG = [cv.take(4096, BF16) for _ in range(2)]
            WU = [cv.take(4096, BF16) for _ in range(2)]
            HB = cv.take(2048, BF16)
            SGB = [cv.take(512, BF16) for _ in range(2)]
            UPB = [cv.take(512, BF16) for _ in range(2)]

            for fc in range(NF):
                b.dma("gpsimd", "sX", XLN2[:, fc * 2048:(fc + 1) * 2048],
                      xln2sp[128 * fc:128 * fc + 128, :])
            xl_tgt = b.cnt["sX"]
            xs_guard = 0
            for t in range(4):
                b.wait("tensor", "sX", xl_tgt)
                b.wait("tensor", "sACT", xs_guard)
                pe = None
                for fc in range(NF):
                    last = (fc == NF - 1)
                    pe = b.mm(ps2[0:32, :], b2sb[:, (fc % 16) * 32:(fc % 16) * 32 + 32],
                              XLN2[:, fc * 2048 + 512 * t: fc * 2048 + 512 * t + 512],
                              start=(fc == 0), stop=last, sem=("sPE" if last else None))
                b.wait("scalar", "sPE", pe)
                xs_guard = b.op("scalar", "copy", xs2sb[:, 512 * t:512 * t + 512], ps2[0:32, :], sem="sACT")
            a_xs2 = xs_guard
            if dbg:
                b.wait("gpsimd", "sACT", a_xs2)
                b.dma("gpsimd", "sX", xs2dump, xs2sb[:])

            w_free = [0, 0]
            gu_free = [[0, 0], [0, 0]]
            hb_dma = 0
            psG = [ps0, ps1]
            for m in range(NI):
                pm = m % 2
                wsem = "sWa" if pm == 0 else "sWb"
                b.wait("sync", "sPE", w_free[pm])
                wgt = b.dma("sync", wsem, WG[pm][:, 0:2048], wgsp[m][:, 0:2048])
                wgt = b.dma("sync", wsem, WG[pm][:, 2048:4096], wgsp[m][:, 2048:4096])
                wut = b.dma("sync", wsem, WU[pm][:, 0:2048], wusp[m][:, 0:2048])
                wut = b.dma("sync", wsem, WU[pm][:, 2048:4096], wusp[m][:, 2048:4096])
                vg_t = b.dma("sync", wsem, vgt[:, pm * 128:pm * 128 + 128], vg_d[:, 128 * m:128 * m + 128])
                vu_t = b.dma("sync", wsem, vut[:, pm * 128:pm * 128 + 128], vu_d[:, 128 * m:128 * m + 128])
                pe_last = None
                for t in range(4):
                    pt = t % 2
                    b.wait("tensor", wsem, max(wgt, vg_t))
                    b.wait("tensor", "sACT", max(a_xs2, gu_free[0][pt]))
                    for f in range(NF):
                        b.mm(psG[pt][:], WG[pm][:, f * 128:(f + 1) * 128],
                             XLN2[:, f * 2048 + 512 * t: f * 2048 + 512 * t + 512],
                             start=(f == 0), stop=False)
                    peg = b.mm(psG[pt][:], vgt[0:32, pm * 128:pm * 128 + 128], xs2sb[:, 512 * t:512 * t + 512],
                               start=False, stop=True, sem="sPE")
                    b.wait("tensor", wsem, max(wut, vu_t))
                    b.wait("tensor", "sACT", gu_free[1][pt])
                    for f in range(NF):
                        b.mm(psA[:, pt * 512:pt * 512 + 512], WU[pm][:, f * 128:(f + 1) * 128],
                             XLN2[:, f * 2048 + 512 * t: f * 2048 + 512 * t + 512],
                             start=(f == 0), stop=False)
                    peu = b.mm(psA[:, pt * 512:pt * 512 + 512], vut[0:32, pm * 128:pm * 128 + 128],
                               xs2sb[:, 512 * t:512 * t + 512], start=False, stop=True, sem="sPE")
                    pe_last = peu
                    b.wait("scalar", "sPE", peu)
                    b.op("scalar", "activation", SGB[pt], psG[pt][:], AF.Silu, bias=0.0, scale=1.0)
                    gu_free[0][pt] = gu_free[1][pt] = aev = b.op(
                        "scalar", "copy", UPB[pt], psA[:, pt * 512:pt * 512 + 512], sem="sACT")
                    if dbg and m == 0:
                        b.wait("gpsimd", "sACT", aev)
                        b.dma("gpsimd", "sX", hdump[:, 1024 * t:1024 * t + 512], SGB[pt])
                        b.dma("gpsimd", "sX", hdump[:, 1024 * t + 512:1024 * t + 1024], UPB[pt])
                        b.wait("vector", "sX", b.cnt["sX"])
                    b.wait("vector", "sACT", aev)
                    b.wait("vector", "sXS", hb_dma)
                    b.op("vector", "tensor_mul", HB[:, 512 * t:512 * t + 512], SGB[pt], UPB[pt],
                         sem="sDVE")
                w_free[pm] = pe_last
                b.wait("gpsimd", "sDVE", b.cnt["sDVE"])
                hb_dma = b.dma("gpsimd", "sXS", hspill[128 * m:128 * m + 128, :], HB)
            b.barrier()

        if upto >= 8:
            cv = Carve(pool)
            HSL = cv.take(NI * 512, BF16)
            WDC = [cv.take(NI * 128, BF16) for _ in range(2)]
            OU1 = [cv.take(512, F32) for _ in range(2)]
            DTb = [cv.take(512, F32) for _ in range(2)]
            FBb = [cv.take(512, F32) for _ in range(2)]

            wd_free = [0, 0]
            dt_free = [0, 0]
            fb_dma = [0, 0]
            ou_free = [0, 0]
            hs_guard = 0
            psD = [ps0, ps1]
            for t in range(4):
                ts0 = 512 * t
                b.wait("gpsimd", "sPE", b.cnt["sPE"])
                hd = 0
                for a, b2_ in ((0, 22), (22, 44), (44, 66), (66, 86)):
                    hd = b.dma("gpsimd", "sX",
                               HSL.rearrange("p (m q) -> p m q", q=512)[:, a:b2_, :],
                               hspill[128 * a:128 * b2_, ts0:ts0 + 512]
                               .rearrange("(m p) q -> p m q", p=128))
                b.wait("tensor", "sX", hd)
                b.wait("tensor", "sACT", hs_guard)
                pe = None
                for m in range(NI):
                    last = (m == NI - 1)
                    pe = b.mm(psA[0:86, 0:512], bdsb[:, (m % 43) * 86:(m % 43) * 86 + 86],
                              HSL[:, m * 512:(m + 1) * 512],
                              start=(m == 0), stop=last, sem=("sPE" if last else None))
                b.wait("scalar", "sPE", pe)
                hs_guard = b.op("scalar", "copy", hssb[:], psA[0:86, 0:512], sem="sACT")
                a_hs = hs_guard
                for hc in range(32):
                    ph = hc % 2
                    wsem8 = "sWa" if ph == 0 else "sWb"
                    b.wait("sync", "sPE", wd_free[ph])
                    wd_t = 0
                    for q4 in range(4):
                        wd_t = b.dma("sync", wsem8, WDC[ph][:, 2752 * q4:2752 * q4 + 2752],
                                     wdsp[hc][:, 2752 * q4:2752 * q4 + 2752])
                    vd_t = b.dma("sync", wsem8, vdt[:, ph * 128:ph * 128 + 128],
                                 vd_d[:, 128 * hc:128 * hc + 128])
                    b.wait("gpsimd", "sDVE", ou_free[ph])
                    ou_t = b.dma("gpsimd", "sQa" if ph == 0 else "sQb",
                                 OU1[ph][:, 0:256], out1sp[128 * hc:128 * hc + 128, ts0:ts0 + 256])
                    ou_t = b.dma("gpsimd", "sQa" if ph == 0 else "sQb",
                                 OU1[ph][:, 256:512], out1sp[128 * hc:128 * hc + 128, ts0 + 256:ts0 + 512])
                    b.wait("tensor", wsem8, max(wd_t, vd_t))
                    b.wait("tensor", "sACT", max(a_hs, dt_free[ph]))
                    for m in range(NI):
                        b.mm(psD[ph][:], WDC[ph][:, m * 128:(m + 1) * 128],
                             HSL[:, m * 512:(m + 1) * 512], start=(m == 0), stop=False)
                    pe_d = b.mm(psD[ph][:], vdt[0:86, ph * 128:ph * 128 + 128], hssb[:],
                                start=False, stop=True, sem="sPE")
                    wd_free[ph] = pe_d
                    b.wait("scalar", "sPE", pe_d)
                    dt_free[ph] = b.op("scalar", "copy", DTb[ph], psD[ph][:], sem="sACT")
                    b.wait("vector", "sACT", dt_free[ph])
                    b.wait("vector", "sQa" if ph == 0 else "sQb", ou_t)
                    b.wait("vector", "sSa" if ph == 0 else "sSb", fb_dma[ph])
                    vf = b.op("vector", "tensor_add", FBb[ph], DTb[ph], OU1[ph], sem="sDVE")
                    ou_free[ph] = vf
                    b.wait("gpsimd", "sDVE", vf)
                    fb_dma[ph] = b.dma("gpsimd", "sSa" if ph == 0 else "sSb",
                                       outT[128 * hc:128 * hc + 128, ts0:ts0 + 512], FBb[ph])

        b.wait("gpsimd", "sX", b.cnt["sX"])
        b.wait("gpsimd", "sX2", b.cnt["sX2"])
        b.wait("gpsimd", "sX3", b.cnt["sX3"])
        b.wait("gpsimd", "sXS", b.cnt["sXS"])
        b.wait("gpsimd", "sX4", b.cnt["sX4"])
        b.wait("gpsimd", "sWa", b.cnt["sWa"])
        b.wait("gpsimd", "sWb", b.cnt["sWb"])
        b.wait("gpsimd", "sWc", b.cnt["sWc"])
        b.wait("gpsimd", "sWd", b.cnt["sWd"])
        b.wait("gpsimd", "sWE", b.cnt["sWE"])
        b.wait("gpsimd", "sQa", b.cnt["sQa"])
        b.wait("gpsimd", "sQb", b.cnt["sQb"])
        b.wait("gpsimd", "sQc", b.cnt["sQc"])
        b.wait("gpsimd", "sQd", b.cnt["sQd"])
        b.wait("gpsimd", "sSa", b.cnt["sSa"])
        b.wait("gpsimd", "sSb", b.cnt["sSb"])
        b.wait("gpsimd", "sDq1", b.cnt["sDq1"])
        b.wait("gpsimd", "sDq2", b.cnt["sDq2"])

        @block.sync
        def _(e):
            for f in b.streams["sync"]:
                f(e)

        @block.vector
        def _(e):
            for f in b.streams["vector"]:
                f(e)

        @block.scalar
        def _(e):
            for f in b.streams["scalar"]:
                f(e)

        @block.tensor
        def _(e):
            for f in b.streams["tensor"]:
                f(e)

        @block.gpsimd
        def _(e):
            for f in b.streams["gpsimd"]:
                f(e)

    return nc


def _pack_shared(wqkv, wo, gqw, gsc, gqz, uqw, usc, uqz, dqw, dsc, dqz):
    wqkv_p = np.ascontiguousarray(wqkv[PERM_H, :])
    wqkv_t = np.ascontiguousarray(
        wqkv_p[:, :5120].reshape(32, 128, 40, 128).transpose(2, 1, 0, 3))
    wv_t = np.ascontiguousarray(wqkv_p[:, 5120:].reshape(32, 128, 1024))
    wo_t = np.ascontiguousarray(wo[:, PERM_H].reshape(32, 128, 32, 128).transpose(2, 1, 0, 3))
    qwg_t = np.ascontiguousarray(
        gqw[PERM_H, :].reshape(32, 128, 43, 32).transpose(2, 1, 0, 3).reshape(43, 128, 1024))
    qwu_t = np.ascontiguousarray(
        uqw[PERM_H, :].reshape(32, 128, 43, 32).transpose(2, 1, 0, 3).reshape(43, 128, 1024))
    qwd_t = np.ascontiguousarray(dqw[PERM_I, :].reshape(86, 128, 512))
    sg_dev = np.ascontiguousarray(gsc[:, PERM_I])
    su_dev = np.ascontiguousarray(usc[:, PERM_I])
    sd_dev = np.ascontiguousarray(dsc[:, PERM_H])

    cidx = np.arange(4096)
    grow = 2 * ((cidx // 128) % 16)
    tcol = cidx % 128
    sgB = np.empty((86, 2, 4096), dtype=np.float32)
    suB = np.empty((86, 2, 4096), dtype=np.float32)
    for m in range(86):
        for hf in range(2):
            sgB[m, hf] = sg_dev[grow + hf, 128 * m + tcol]
            suB[m, hf] = su_dev[grow + hf, 128 * m + tcol]

    p = np.arange(128)
    b2 = np.zeros((128, 512), dtype=np.float32)
    for j in range(16):
        b2[p, j * 32 + 2 * j + (p >= 64)] = 1.0
    bd = np.zeros((128, 43 * 86), dtype=np.float32)
    for j in range(43):
        bd[p, j * 86 + 2 * j + (p >= 64)] = 1.0
    cstv = np.zeros((1, 16), dtype=np.float32)
    cstv[0, :len(CONSTS)] = CONSTS

    return {
        "wqkv_t": wqkv_t, "wv_t": wv_t, "wo_t": wo_t,
        "qwg_t": qwg_t, "qwu_t": qwu_t, "qwd_t": qwd_t,
        "sg_dev": sg_dev, "su_dev": su_dev, "sd_dev": sd_dev,
        "sgB_t": sgB, "suB_t": suB,
        "qzg": np.ascontiguousarray(gqz), "qzu": np.ascontiguousarray(uqz),
        "qzd": np.ascontiguousarray(dqz),
        "b2_in": b2.astype(BF), "bd_in": bd.astype(BF),
        "id_in": np.eye(128, dtype=np.float32).astype(BF),
        "cst_in": cstv,
    }


def _sdm_table(half):
    out = np.empty((NKT, 128, 512), dtype=np.float32)
    for pos, blk in enumerate(MYBLOCKS[half]):
        q0 = 512 * blk
        for kt in range(PADDED[pos]):
            k = 128 * kt + np.arange(128)[:, None]
            q = q0 + np.arange(512)[None, :]
            rel = q - k
            v = np.where(rel >= 0, np.sqrt(np.maximum(rel, 0).astype(np.float64)), 1e30)
            out[CUMPAD[pos] + kt] = v.astype(np.float32)
    return out.astype(BF)


def make_inmaps(x, wqkv, wo, gqw, gsc, gqz, uqw, usc, uqz, dqw, dsc, dqz, ncore=8):
    shared = _pack_shared(wqkv, wo, gqw, gsc, gqz, uqw, usc, uqz, dqw, dsc, dqz)
    sdm = [_sdm_table(h) for h in range(2)]
    in_maps = []
    my_toks = []
    for c in range(ncore):
        bb, hh = c // 2, c % 2
        xT = x[bb].T[PERM_H, :]
        my_tok = np.concatenate([np.arange(512 * blk, 512 * blk + 512) for blk in MYBLOCKS[hh]])
        my_toks.append(my_tok)
        m = dict(shared)
        m["x_batchT"] = np.ascontiguousarray(xT.astype(np.float32))
        m["x_myT"] = np.ascontiguousarray(xT[:, my_tok].astype(np.float32))
        m["sdm_t"] = sdm[hh]
        in_maps.append(m)
    return in_maps, my_toks


def kernel(L0_attn_hidden_states, L0_attn_w_qkv, L0_attn_w_o,
           L0_ffn_gate_qweight, L0_ffn_gate_scales, L0_ffn_gate_qzeros,
           L0_ffn_up_qweight, L0_ffn_up_scales, L0_ffn_up_qzeros,
           L0_ffn_down_qweight, L0_ffn_down_scales, L0_ffn_down_qzeros):
    in_maps, my_toks = make_inmaps(
        np.asarray(L0_attn_hidden_states),
        np.asarray(L0_attn_w_qkv), np.asarray(L0_attn_w_o),
        np.asarray(L0_ffn_gate_qweight), np.asarray(L0_ffn_gate_scales),
        np.asarray(L0_ffn_gate_qzeros),
        np.asarray(L0_ffn_up_qweight), np.asarray(L0_ffn_up_scales),
        np.asarray(L0_ffn_up_qzeros),
        np.asarray(L0_ffn_down_qweight), np.asarray(L0_ffn_down_scales),
        np.asarray(L0_ffn_down_qzeros))

    nc = build_program(upto=8)
    res = run_bass_kernel_spmd(nc, in_maps, core_ids=list(range(8)))

    out = np.empty((B, S, H), dtype=np.float32)
    for c in range(8):
        bb = c // 2
        o = np.asarray(res.results[c]["outT"])
        tmp = np.empty((H, TOK), dtype=np.float32)
        tmp[PERM_H, :] = o
        out[bb, my_toks[c], :] = tmp.T
    return out
